# version 09
# baseline (speedup 1.0000x reference)
import contextlib
import numpy as np
import concourse.bass as bass
import concourse.mybir as mybir
from concourse.bass_utils import run_bass_kernel_spmd

F32 = mybir.dt.float32
F16 = mybir.dt.float16
BF16 = mybir.dt.bfloat16
AL = mybir.AluOpType
ACT = mybir.ActivationFunctionType
AX = mybir.AxisListType

D = 1024
DI = 2048
L = 2048
NS = 16
NE = 15
HF = 1024
ND = D // 128
NG = DI // 128
NH = HF // 128
TC = 256
NCH = L // TC
SC = 512
NSC = L // SC
TF = 512
NTF = L // TF
EPS = 1e-5

_DEBUG_TAPS = []
_STOP = [99]


class Tok:
    def __init__(self, handle):
        self.h = handle
        self.n = 0

    def inc(self, inst, k=1):
        inst.then_inc(self.h, k)
        self.n += k
        return self.n


def build_program():
    nc = bass.Bass("TRN2", target_bir_lowering=False)

    def din(name, shape, dt=F32):
        return nc.dram_tensor(name, list(shape), dt, kind="ExternalInput").ap()

    x_in = din("x", (D, L))
    w_in = din("w_in", (D, 2 * DI))
    w_conv = din("w_conv", (128, NG, 4))
    conv_b = din("conv_b", (128, NG))
    w_xp = din("w_xp", (128, NG, 33))
    dt_w = din("dt_w", (128, NG))
    dt_b = din("dt_b", (128, NG))
    a_log = din("a_log", (128, NG, NS))
    d_par = din("d_par", (128, NG))
    w_out = din("w_out", (DI, D))
    r_w = din("r_w", (128, ND, NE))
    r_b = din("r_b", (1, NE))
    s_w1 = din("s_w1", (D, HF))
    s_b1 = din("s_b1", (128, NH))
    s_w2 = din("s_w2", (HF, D))
    s_b2 = din("s_b2", (1, D))
    e_w1 = din("e_w1", (NE, D, HF))
    e_b1 = din("e_b1", (128, NE, NH))
    e_w2 = din("e_w2", (NE, HF, D))
    e_b2 = din("e_b2", (NE, D))
    cst = din("cst", (128, 4))
    ones_r = din("ones_r", (1, TF))
    ones_rb = din("ones_rb", (1, TF), BF16)
    ones_cb = din("ones_cb", (128, 1), BF16)
    eye = din("eye", (128, 128))
    sel = din("sel", (33, 33 * 128), BF16)

    out_t = nc.dram_tensor("out", [D, L], F32, kind="ExternalOutput").ap()
    taps = {}
    tap_shapes = {
        "xpre0": [128, NG, TC], "xpost0": [128, NG, SC],
        "bcd0": [33, 2, TC], "yg0": [128, NG, SC], "x2": [128, ND, L],
        "cw0": [128, NE], "u0": [128, ND, SC],
    }
    for tap in _DEBUG_TAPS:
        taps[tap] = nc.dram_tensor("tap_" + tap, tap_shapes[tap], F32,
                                   kind="ExternalOutput").ap()

    scr_in = nc.dram_tensor("scr_in", [8, 8, 128, 512], BF16, kind="Internal").ap()
    scr_out = nc.dram_tensor("scr_out", [2, 16, 128, 512], BF16, kind="Internal").ap()
    scr_e = nc.dram_tensor("scr_e", [16, 2, 8, 128, 1024], BF16, kind="Internal").ap()

    st = contextlib.ExitStack()
    with st:
        def sb(name, shape, dt=F32):
            return st.enter_context(nc.sbuf_tensor(name, list(shape), dt))

        def sem(name):
            return Tok(st.enter_context(nc.semaphore(name)))

        d_sy = sem("d_sy")
        d_gp = sem("d_gp")
        c_t = sem("c_t")
        c_s = sem("c_s")
        c_v = sem("c_v")
        c_g = sem("c_g")
        c_o0 = sem("c_o0")
        c_o1 = sem("c_o1")
        d_w0 = sem("d_w0")
        d_w1 = sem("d_w1")
        d_ce = sem("d_ce")
        d_f0 = sem("d_f0")
        d_f1 = sem("d_f1")
        d_f2 = sem("d_f2")

        k_conv_w = sb("k_conv_w", (128, NG, 4))
        k_conv_b = sb("k_conv_b", (128, NG))
        k_dtw = sb("k_dtw", (128, NG))
        k_dtb = sb("k_dtb", (128, NG))
        k_a = sb("k_a", (128, NG, NS))
        k_d = sb("k_d", (128, NG))
        k_sb1 = sb("k_sb1", (128, NH))
        k_eb1 = sb("k_eb1", (128, NE, NH))
        k_sb2b = sb("k_sb2b", (1, D), BF16)
        k_eb2b = sb("k_eb2b", (NE, D), BF16)
        k_rb_b = sb("k_rb_b", (1, NE), BF16)
        k_rw_b = sb("k_rw_b", (128, ND, NE), BF16)
        k_xp_b = sb("k_xp_b", (128, NG, 33), BF16)
        k_ones_r = sb("k_ones_r", (1, 128))
        k_ones_h = sb("k_ones_h", (1, 128), F16)
        k_ones_rb = sb("k_ones_rb", (1, TF), BF16)
        k_ones_cb = sb("k_ones_cb", (128, 1), BF16)
        k_eye = sb("k_eye", (128, 128))
        k_sel = sb("k_sel", (33, 33 * 128), BF16)
        k_cst = sb("k_cst", (128, 4))
        nc.const_aps.register(F32, EPS, k_cst[:, 0:1])
        nc.const_aps.register(F32, 1.0, k_cst[:, 1:2])
        nc.const_aps.register(F32, -0.5, k_cst[:, 2:3])
        nc.const_aps.register(F32, 0.0, k_cst[:, 3:4])


        block = st.enter_context(nc.Block())
        tk = {}

        kscope = contextlib.ExitStack()
        k_st1 = kscope.enter_context(nc.sbuf_tensor("k_st1", [128, NG, NS], F32))
        k_st2 = kscope.enter_context(nc.sbuf_tensor("k_st2", [128, NG, 33], F32))
        k_st3 = kscope.enter_context(nc.sbuf_tensor("k_st3", [128, ND, NE], F32))
        k_st4 = kscope.enter_context(nc.sbuf_tensor("k_st4", [NE, D], F32))
        k_st5 = kscope.enter_context(nc.sbuf_tensor("k_st5", [1, NE], F32))
        k_st6 = kscope.enter_context(nc.sbuf_tensor("k_st6", [1, D], F32))

        def ph_c_sync(sy):
            for dst, src in [
                (k_conv_w[:], w_conv), (k_conv_b[:], conv_b), (k_dtw[:], dt_w),
                (k_dtb[:], dt_b), (k_st1[:], a_log), (k_d[:], d_par),
                (k_sb1[:], s_b1), (k_eb1[:], e_b1),
                (k_ones_r[:], ones_r[0:1, 0:128]), (k_ones_rb[:], ones_rb),
                (k_ones_cb[:], ones_cb), (k_eye[:], eye),
                (k_st2[:], w_xp), (k_st3[:], r_w),
                (k_st4[:], e_b2), (k_st5[:], r_b), (k_st6[:], s_b2),
                (k_cst[:], cst), (k_sel[:], sel),
            ]:
                sy.dma_start(dst, src).then_inc(d_sy.h, 16)
                d_sy.n += 16
            tk["consts"] = d_sy.n
        block.sync(ph_c_sync)

        def ph_c_scalar(s):
            s.wait_ge(d_sy.h, tk["consts"])
            c_s.inc(s.activation(k_a[:], k_st1[:], ACT.Exp))
            c_s.inc(s.copy(k_xp_b[:], k_st2[:]))
            c_s.inc(s.copy(k_ones_h[:], k_ones_r[:]))
            c_s.inc(s.copy(k_rw_b[:], k_st3[:]))
            c_s.inc(s.copy(k_eb2b[:], k_st4[:]))
            c_s.inc(s.copy(k_rb_b[:], k_st5[:]))
            tk["sc_consts"] = c_s.inc(s.copy(k_sb2b[:], k_st6[:]))
        block.scalar(ph_c_scalar)

        def ph_c_vec(v):
            v.wait_ge(c_s.h, tk["sc_consts"])
            tk["a_neg"] = c_v.inc(v.tensor_scalar(k_a[:], k_a[:], -1.0, None, AL.mult))
        block.vector(ph_c_vec)
        kscope.close()

        cscope = contextlib.ExitStack()
        with cscope:
            cast_stage = cscope.enter_context(nc.sbuf_tensor("cast_stage", [128, 4096], F32))
            cast_bf = cscope.enter_context(nc.sbuf_tensor("cast_bf", [128, 4096], BF16))

            for i in range(24):
                tk[("cast", i)] = c_s.n + i + 1
            ndma_per = [8] * 8 + [2] * 16
            acc = d_sy.n
            for i in range(24):
                acc += 16 * ndma_per[i]
                tk[("cscr", i)] = acc

            def ph_cast_gp(g):
                g.wait_ge(c_s.h, tk["sc_consts"])
                g.wait_ge(c_v.h, tk["a_neg"])
                for i in range(24):
                    if i >= 1:
                        g.wait_ge(c_s.h, tk[("cast", i - 1)])
                    if i < 8:
                        g.dma_start(cast_stage[:],
                                    w_in[i * 128:(i + 1) * 128, :]).then_inc(d_gp.h, 16)
                    else:
                        g.dma_start(cast_stage[:, 0:1024],
                                    w_out[(i - 8) * 128:(i - 7) * 128, :]).then_inc(d_gp.h, 16)
                    d_gp.n += 16
                    tk[("cdma", i)] = d_gp.n
            block.gpsimd(ph_cast_gp)

            def ph_cast_s(s):
                for i in range(24):
                    s.wait_ge(d_gp.h, tk[("cdma", i)])
                    if i >= 1:
                        s.wait_ge(d_sy.h, tk[("cscr", i - 1)])
                    if i < 8:
                        inst = s.copy(cast_bf[:], cast_stage[:])
                    else:
                        inst = s.copy(cast_bf[:, 0:1024], cast_stage[:, 0:1024])
                    assert tk[("cast", i)] == c_s.inc(inst)
            block.scalar(ph_cast_s)

            def ph_cast_sy(sy):
                for i in range(24):
                    sy.wait_ge(c_s.h, tk[("cast", i)])
                    ngrp = 8 if i < 8 else 2
                    for mg in range(ngrp):
                        dst = scr_in[mg, i] if i < 8 else scr_out[mg, i - 8]
                        sy.dma_start(dst, cast_bf[:, mg * 512:(mg + 1) * 512]).then_inc(d_sy.h, 16)
                        d_sy.n += 16
                    assert tk[("cscr", i)] == d_sy.n
                tk["scr_done"] = d_sy.n
            block.sync(ph_cast_sy)
            tk["cast_gp_done"] = d_gp.n

        ms = contextlib.ExitStack()
        with ms:
            def msb(name, shape, dt=F32):
                return ms.enter_context(nc.sbuf_tensor(name, list(shape), dt))

            xc = msb("xc", (128, 2, ND, TC), BF16)
            m_xln = msb("m_xln", (128, ND, TC), BF16)
            w_kt = msb("w_kt", (128, 2, 512), BF16)
            u_pre = msb("u_pre", (128, 4096), BF16)
            x_carry = msb("x_carry", (128, NG, 4), BF16)
            conv_acc = msb("conv_acc", (128, 2, TC), F16)
            x_post = msb("x_post", (128, NG, SC), BF16)
            z_s = msb("z_s", (128, NG, SC), BF16)
            y_g = z_s
            bcdT = msb("bcdT", (33, 2, TC), BF16)
            m_dtg = msb("m_dtg", (128, SC), F16)
            m_wdt = msb("m_wdt", (128, SC), F16)
            brep = msb("brep", (128, NS, SC), F16)
            crep = msb("crep", (128, NS, SC), F16)
            m_da = msb("m_da", (128, 2, SC), F16)
            m_dbx = msb("m_dbx", (128, 2, SC), F16)
            h_buf = msb("h_buf", (128, NS, SC), F16)
            h_last = msb("h_last", (128, NG, NS), F16)
            y_tmp = msb("y_tmp", (128, SC), F16)
            m_dtr = msb("m_dtr", (128, SC), F16)
            f_xln2 = msb("f_xln2", (128, ND, TF), BF16)
            f_sq = msb("f_sq", (128, ND, TF), BF16)
            f_wb = msb("f_wb", (128, 3, 4, 1024), BF16)
            f_he = msb("f_he", (128, NH, TF), BF16)
            f_acc = msb("f_acc", (128, ND, TF), BF16)
            x2s = msb("x2s", (128, ND, SC), BF16)
            f_cwTb = msb("f_cwTb", (NE, TF), BF16)
            f_prob = msb("f_prob", (128, NE))
            f_q = msb("f_q", (128, NE))
            f_m8 = msb("f_m8", (128, 8))
            f_mt = msb("f_mt", (128, 8))
            f_cw = msb("f_cw", (128, NE))
            f_cwrep = msb("f_cwrep", (128, 2, TF), BF16)
            f_out = msb("f_out", (128, 2, TF), BF16)
            f_ra = msb("f_ra", (1, TF))
            f_rb = msb("f_rb", (1, TF))
            f_rc = msb("f_rc", (1, TF))
            f_col = msb("f_col", (128, 1))
            rowa = msb("rowa", (1, TC))
            rowb = msb("rowb", (1, TC), F16)
            rowc = msb("rowc", (1, TC), F16)

            p_proj = ms.enter_context(nc.psum_tensor("p_proj", [128, 2048], F32))
            p_rep = ms.enter_context(nc.psum_tensor("p_rep", [128, 512], F32))
            p_rep2 = ms.enter_context(nc.psum_tensor("p_rep2", [128, 512], F32))
            p_st = ms.enter_context(nc.psum_tensor("p_st", [128, 512], F32))
            p_dtr = ms.enter_context(nc.psum_tensor("p_dtr", [128, 512], F32))

            def xp(g, a, b):
                return u_pre[:, g * TC + a:g * TC + b]

            def mu(dd, a, b):
                return u_pre[:, dd * SC + a:dd * SC + b]

            wkt_used = {0: None, 1: None}
            rep_free = None
            stat_free = None

            def ln_emit(tag, idx, srcs_bf, srcs_sq, dim, sq_tok):
                nonlocal rep_free

                def f_t(t):
                    t.wait_ge(sq_tok[0].h, sq_tok[1])
                    if rep_free is not None:
                        t.wait_ge(rep_free[0].h, rep_free[1])
                    nk = len(srcs_bf)
                    for k in range(nk):
                        inst = t.matmul(p_st[0:1, 0:TC], k_ones_cb[:], srcs_bf[k],
                                        start=(k == 0), stop=(k == nk - 1))
                    c_t.inc(inst)
                    for k in range(nk):
                        inst = t.matmul(p_st[0:1, TC:2 * TC], k_ones_cb[:], srcs_sq[k],
                                        start=(k == 0), stop=(k == nk - 1))
                    tk[("stat", tag, idx)] = c_t.inc(inst)
                block.tensor(f_t)

                def f_v(v):
                    v.wait_ge(c_t.h, tk[("stat", tag, idx)])
                    t1 = c_v.inc(v.tensor_scalar(rowa[0:1, 0:TC], p_st[0:1, 0:TC],
                                                 1.0 / dim, None, AL.mult))
                    c_v.inc(v.tensor_scalar(rowb[0:1, 0:TC], p_st[0:1, TC:2 * TC],
                                            1.0 / dim, None, AL.mult))
                    v.wait_ge(c_v.h, t1)
                    t3 = c_v.inc(v.tensor_mul(rowc[0:1, 0:TC], rowa[0:1, 0:TC], rowa[0:1, 0:TC]))
                    v.wait_ge(c_v.h, t3)
                    tk[("var", tag, idx)] = c_v.inc(
                        v.tensor_sub(rowb[0:1, 0:TC], rowb[0:1, 0:TC], rowc[0:1, 0:TC]))
                block.vector(f_v)

                def f_s(s):
                    s.wait_ge(c_v.h, tk[("var", tag, idx)])
                    t1 = c_s.inc(s.activation(rowb[0:1, 0:TC], rowb[0:1, 0:TC], ACT.Ln, bias=EPS))
                    s.wait_ge(c_s.h, t1)
                    tk[("rstd", tag, idx)] = c_s.inc(
                        s.activation(rowb[0:1, 0:TC], rowb[0:1, 0:TC], ACT.Exp, scale=-0.5))
                block.scalar(f_s)

                def f_v2(v):
                    v.wait_ge(c_s.h, tk[("rstd", tag, idx)])
                    t1 = c_v.inc(v.tensor_mul(rowc[0:1, 0:TC], rowa[0:1, 0:TC], rowb[0:1, 0:TC]))
                    v.wait_ge(c_v.h, t1)
                    tk[("mrs", tag, idx)] = c_v.inc(
                        v.tensor_scalar(rowc[0:1, 0:TC], rowc[0:1, 0:TC], -1.0, None, AL.mult))
                block.vector(f_v2)

                def f_t2(t):
                    t.wait_ge(c_v.h, tk[("mrs", tag, idx)])
                    c_t.inc(t.matmul(p_rep[:, 0:TC], k_ones_h[0:1, 0:128], rowb[0:1, 0:TC],
                                     start=True, stop=True))
                    tk[("reps", tag, idx)] = c_t.inc(
                        t.matmul(p_rep2[:, 0:TC], k_ones_h[0:1, 0:128], rowc[0:1, 0:TC],
                                 start=True, stop=True))
                block.tensor(f_t2)


            d_fs = [d_f0, d_f1, d_f2]
            fwb_use = {0: None, 1: None, 2: None}
            fwb_tok = {}

            def ffn_rows2(tag, t):
                def fv1(v, tag=tag, t=t):
                    v.wait_ge(c_t.h, tk[("fstat", tag, t)])
                    t1 = c_v.inc(v.tensor_scalar(f_ra[0:1, :], p_st[0:1, 0:TF],
                                                 1.0 / D, None, AL.mult))
                    c_v.inc(v.tensor_scalar(f_rb[0:1, :], p_dtr[0:1, 0:TF],
                                            1.0 / D, None, AL.mult))
                    v.wait_ge(c_v.h, t1)
                    t3 = c_v.inc(v.tensor_mul(f_rc[0:1, :], f_ra[0:1, :], f_ra[0:1, :]))
                    v.wait_ge(c_v.h, t3)
                    tk[("fvar", tag, t)] = c_v.inc(
                        v.tensor_sub(f_rb[0:1, :], f_rb[0:1, :], f_rc[0:1, :]))
                block.vector(fv1)

                def fs1(s, tag=tag, t=t):
                    s.wait_ge(c_v.h, tk[("fvar", tag, t)])
                    t1 = c_s.inc(s.activation(f_rb[0:1, :], f_rb[0:1, :], ACT.Ln, bias=EPS))
                    s.wait_ge(c_s.h, t1)
                    tk[("frstd", tag, t)] = c_s.inc(
                        s.activation(f_rb[0:1, :], f_rb[0:1, :], ACT.Exp, scale=-0.5))
                block.scalar(fs1)

                def fv2(v, tag=tag, t=t):
                    v.wait_ge(c_s.h, tk[("frstd", tag, t)])
                    t1 = c_v.inc(v.tensor_mul(f_rc[0:1, :], f_ra[0:1, :], f_rb[0:1, :]))
                    v.wait_ge(c_v.h, t1)
                    tk[("fmrs", tag, t)] = c_v.inc(
                        v.tensor_scalar(f_rc[0:1, :], f_rc[0:1, :], -1.0, None, AL.mult))
                block.vector(fv2)

                def ft2(t_, tag=tag, t=t):
                    t_.wait_ge(c_v.h, tk[("fmrs", tag, t)])
                    if tag == "ln3":
                        if t + 1 <= 3:
                            t_.wait_ge(c_v.h, tk[("xln", 2 * (t + 1) + 1)])
                        else:
                            t_.wait_ge(c_v.h, tk[("x2done", 3)])
                    else:
                        t_.wait_ge(c_v.h, tk[("xln2", t)])
                    c_t.inc(t_.matmul(p_rep[:, 0:TF], k_ones_r[0:1, 0:128], f_rb[0:1, :],
                                      start=True, stop=True))
                    tk[("freps", tag, t)] = c_t.inc(
                        t_.matmul(p_rep2[:, 0:TF], k_ones_r[0:1, 0:128], f_rc[0:1, :],
                                  start=True, stop=True))
                block.tensor(ft2)

            def emit_ffn_pre(t, s):
                def f3_sq(sc_, t=t, s=s):
                    if t >= 1:
                        sc_.wait_ge(c_t.h, tk[("fstat", "ln4", t - 1)])
                        sc_.wait_ge(c_t.h, tk[("stat", "ln2", 2 * t + 1)])
                    if s <= 3:
                        sc_.wait_ge(c_t.h, tk[("stat", "ln1", 2 * s + 1)])
                    sc_.wait_ge(c_v.h, tk[("x2done", t)])
                    for k in range(ND):
                        c_s.inc(sc_.activation(f_sq[:, k, :], x2s[:, k, :], ACT.Square))
                    tk[("sq3", t)] = c_s.n
                block.scalar(f3_sq)

                def f3_stats(t_, t=t, s=s):
                    t_.wait_ge(c_s.h, tk[("sq3", t)])
                    if s <= 3:
                        t_.wait_ge(c_v.h, tk[("var", "ln1", 2 * s + 1)])
                        t_.wait_ge(c_s.h, tk[("dtrev", s)])
                    else:
                        t_.wait_ge(c_v.h, tk[("var", "ln2", 2 * t + 1)])
                        t_.wait_ge(c_v.h, tk[("fvar", "ln4", t - 1)])
                    for k in range(ND):
                        inst = t_.matmul(p_st[0:1, 0:TF], k_ones_cb[:], x2s[:, k, :],
                                         start=(k == 0), stop=(k == ND - 1))
                    c_t.inc(inst)
                    for k in range(ND):
                        inst = t_.matmul(p_dtr[0:1, 0:TF], k_ones_cb[:], f_sq[:, k, :],
                                         start=(k == 0), stop=(k == ND - 1))
                    tk[("fstat", "ln3", t)] = c_t.inc(inst)
                block.tensor(f3_stats)

                ffn_rows2("ln3", t)

                _vn3 = c_v.n

                def f3_norm(v, t=t, _vn3=_vn3):
                    v.wait_ge(c_v.h, _vn3)
                    v.wait_ge(c_t.h, tk[("freps", "ln3", t)])
                    if t >= 1:
                        v.wait_ge(c_t.h, tk[("w1done", t - 1, 15)])
                    t1 = None
                    for k in range(ND):
                        if t1 is not None:
                            v.wait_ge(c_v.h, t1)
                        t1 = c_v.inc(v.tensor_mul(y_tmp[:], x2s[:, k, :], p_rep[:, 0:TF]))
                        v.wait_ge(c_v.h, t1)
                        t1 = c_v.inc(v.tensor_add(f_xln2[:, k, :], y_tmp[:],
                                                  p_rep2[:, 0:TF]))
                    tk[("xln2", t)] = c_v.n
                block.vector(f3_norm)

                for tt in range(4):
                    def fr_mm(t_, t=t, tt=tt):
                        t_.wait_ge(c_v.h, tk[("xln2", t)])
                        if tt == 0:
                            t_.wait_ge(c_v.h, tk[("fvar", "ln3", t)])
                        prev = tk.get(("cwTevac", t, tt - 1)) if tt > 0 else \
                            tk.get(("cwTevac", t - 1, 3))
                        if prev is not None:
                            t_.wait_ge(c_s.h, prev)
                        for k in range(ND):
                            t_.matmul(p_dtr[0:128, 0:NE],
                                      f_xln2[:, k, tt * 128:(tt + 1) * 128],
                                      k_rw_b[:, k, :], start=(k == 0), stop=False)
                        tk[("logit", t, tt)] = c_t.inc(
                            t_.matmul(p_dtr[0:128, 0:NE], k_ones_rb[0:1, 0:128],
                                      k_rb_b[0:1, :], start=False, stop=True))
                    block.tensor(fr_mm)

                    def fr_sc(s_, t=t, tt=tt):
                        s_.wait_ge(c_t.h, tk[("logit", t, tt)])
                        tk[("loge", t, tt)] = c_s.inc(s_.copy(f_prob[:], p_dtr[0:128, 0:NE]))
                    block.scalar(fr_sc)

                    def fr_max(v, t=t, tt=tt):
                        v.wait_ge(c_s.h, tk[("loge", t, tt)])
                        tk[("rmax", t, tt)] = c_v.inc(
                            v.reduce_max(f_col[:, 0:1], f_prob[:], AX.X, negate=True))
                    block.vector(fr_max)

                    def fr_exp(s_, t=t, tt=tt):
                        s_.wait_ge(c_v.h, tk[("rmax", t, tt)])
                        tk[("rexp", t, tt)] = c_s.inc(
                            s_.activation(f_prob[:], f_prob[:], ACT.Exp, bias=f_col[:, 0:1]))
                    block.scalar(fr_exp)

                    def fr_cw(v, t=t, tt=tt):
                        v.wait_ge(c_s.h, tk[("rexp", t, tt)])
                        if "tapcw" in tk and tk.get(("tapcw_w",)) is None:
                            tk[("tapcw_w",)] = 1
                            v.wait_ge(d_gp.h, tk["tapcw"])
                        t1 = c_v.inc(v.reduce_sum(f_col[:, 0:1], f_prob[:], AX.X))
                        v.wait_ge(c_v.h, t1)
                        t1 = c_v.inc(v.reciprocal(f_col[:, 0:1], f_col[:, 0:1]))
                        v.wait_ge(c_v.h, t1)
                        t1 = c_v.inc(v.tensor_scalar(f_prob[:], f_prob[:], f_col[:, 0:1],
                                                     None, AL.mult))
                        v.wait_ge(c_v.h, t1)
                        t1 = c_v.inc(v.max(f_m8[:], f_prob[:]))
                        c_v.inc(v.memset(f_mt[:], -5.0))
                        v.wait_ge(c_v.h, c_v.n)
                        t1 = c_v.inc(v.tensor_copy(f_mt[:, 0:3], f_m8[:, 0:3]))
                        v.wait_ge(c_v.h, t1)
                        t1 = c_v.inc(v.match_replace(f_q[:], f_mt[:], f_prob[:], 0.0))
                        v.wait_ge(c_v.h, t1)
                        tk[("cw", t, tt)] = c_v.inc(
                            v.tensor_sub(f_cw[:], f_prob[:], f_q[:]))
                    block.vector(fr_cw)

                    def fr_tp(t_, t=t, tt=tt):
                        t_.wait_ge(c_v.h, tk[("cw", t, tt)])
                        tk[("cwT", t, tt)] = c_t.inc(
                            t_.transpose(p_dtr[0:NE, 128:256], f_cw[:], k_eye[:]))
                    block.tensor(fr_tp)

                    def fr_tpe(s_, t=t, tt=tt):
                        s_.wait_ge(c_t.h, tk[("cwT", t, tt)])
                        if t >= 1 and tt == 0:
                            s_.wait_ge(c_t.h, tk[("w2done", t - 1, 15)])
                        tk[("cwTevac", t, tt)] = c_s.inc(
                            s_.copy(f_cwTb[:, tt * 128:(tt + 1) * 128],
                                    p_dtr[0:NE, 128:256]))
                    block.scalar(fr_tpe)

                if "cw0" in taps and t == 0:
                    def f_tap_cw(g):
                        g.wait_ge(c_v.h, tk[("cw", 0, 3)])
                        g.wait_ge(d_gp.h, d_gp.n)
                        g.dma_start(taps["cw0"], f_cw[:]).then_inc(d_gp.h, 16)
                        d_gp.n += 16
                        tk["tapcw"] = d_gp.n
                    block.gpsimd(f_tap_cw)

            def relu_layout(t, snap, interleaved):
                dt18 = 18 if interleaved else 0

                def relub(e):
                    return 8 + (1 if e >= 1 else 0)
                for e in range(16):
                    start = snap + dt18 * (e + 1) + sum(relub(m) for m in range(e))
                    for hmg in range(4):
                        for j in range(2):
                            tk[("reluh", t, e, hmg, j)] = start + 2 * hmg + j + 1
                    if e >= 1:
                        tk[("cwrepe", t, e)] = start + 9
                    tk[("reludone", t, e)] = start + relub(e)

            def acc_layout(t, base, interleaved):
                pos = base
                for k in range(NG):
                    pos += 45 if interleaved else 0
                    if k >= 2:
                        tk[("scale", t, k - 1)] = pos + 1
                        pos += 1
                        for dmg in range(4):
                            for j in range(2):
                                tk[("accvh", t, k - 2, dmg, j)] = pos + 2 * dmg + j + 1
                        pos += 8

            def emit_fwa(t, e):
                def fw(sy, t=t, e=e):
                    sy.wait_ge(d_ce.h, EXPCAST)
                    for (wi, hh) in ((0, 0), (0, 1), (1, 0)):
                        hs = (4 * e + 2 * wi + hh) % 3
                        df = d_fs[hs]
                        if fwb_use[hs] is not None:
                            sy.wait_ge(c_t.h, fwb_use[hs])
                        for kk in range(4):
                            for ho in range(2):
                                sy.dma_start(f_wb[:, hs, kk, ho * 512:(ho + 1) * 512],
                                             scr_e[e, wi, hh * 4 + kk][:, ho * 512:(ho + 1) * 512]
                                             ).then_inc(df.h, 16)
                                df.n += 16
                        fwb_tok[(t, e, wi, hh)] = df.n
                block.sync(fw)

            def emit_fwb(t, e):
                def fw(sy, t=t, e=e):
                    hs = (4 * e + 3) % 3
                    df = d_fs[hs]
                    if fwb_use[hs] is not None:
                        sy.wait_ge(c_t.h, fwb_use[hs])
                    for kk in range(4):
                        for ho in range(2):
                            sy.dma_start(f_wb[:, hs, kk, ho * 512:(ho + 1) * 512],
                                         scr_e[e, 1, 4 + kk][:, ho * 512:(ho + 1) * 512]
                                         ).then_inc(df.h, 16)
                            df.n += 16
                    fwb_tok[(t, e, 1, 1)] = df.n
                block.sync(fw)

            def emit_cwrep(t, e):
                if e == 0:
                    return
                cwrow = e - 1

                def f(t_, t=t, e=e, cwrow=cwrow):
                    if e == 1:
                        t_.wait_ge(c_v.h, tk[("fvar", "ln3", t)])
                    else:
                        t_.wait_ge(c_s.h, tk[("cwrepe2", t, e - 1)])
                    t_.wait_ge(c_s.h, tk[("cwTevac", t, 3)])
                    tk[("cwrepps", t, e)] = c_t.inc(
                        t_.matmul(p_st[:, 0:TF],
                                  k_sel[0:NE, cwrow * 128:(cwrow + 1) * 128],
                                  f_cwTb[:], start=True, stop=True))
                block.tensor(f)

            def emit_w1(t, e, s):
                def f(t_, t=t, e=e, s=s):
                    hsA = (4 * e) % 3
                    hsB = (4 * e + 1) % 3
                    t_.wait_ge(d_fs[hsA].h, fwb_tok[(t, e, 0, 0)])
                    t_.wait_ge(d_fs[hsB].h, fwb_tok[(t, e, 0, 1)])
                    t_.wait_ge(c_v.h, tk[("xln2", t)])
                    if e == 0:
                        if s <= 3:
                            t_.wait_ge(c_s.h, tk[("evac", 2 * s + 1, 7)])
                        else:
                            t_.wait_ge(c_s.h, tk[("uevac", 3, 1)])
                    for hmg in range(4):
                        for j in range(2):
                            if hmg >= 1:
                                t_.wait_ge(c_s.h, tk[("reluh", t, e, hmg - 1, j)])
                            elif e >= 1:
                                t_.wait_ge(c_s.h, tk[("reluh", t, e - 1, 3, j)])
                            for k in range(8):
                                hs = hsA if k < 4 else hsB
                                inst = t_.matmul(
                                    p_proj[:, j * TF:(j + 1) * TF],
                                    f_wb[:, hs, k % 4,
                                         (hmg * 2 + j) * 128:(hmg * 2 + j + 1) * 128],
                                    f_xln2[:, k, :], start=(k == 0), stop=(k == 7))
                            c_t.inc(inst)
                            tk[("w1ps", t, e, hmg, j)] = c_t.n
                    tk[("w1done", t, e)] = c_t.n
                    fwb_use[hsA] = c_t.n
                    fwb_use[hsB] = c_t.n
                block.tensor(f)

            def emit_relu(t, e):
                def f(s_, t=t, e=e):
                    b1 = e_srcs[e][2]
                    for hmg in range(4):
                        for j in range(2):
                            m = hmg * 2 + j
                            s_.wait_ge(c_t.h, tk[("w1ps", t, e, hmg, j)])
                            if hmg == 0 and e >= 1:
                                s_.wait_ge(c_t.h, tk[("w2done", t, e - 1)])
                            assert tk[("reluh", t, e, hmg, j)] == c_s.inc(
                                s_.activation(f_he[:, m, :],
                                              p_proj[:, j * TF:(j + 1) * TF],
                                              ACT.Relu, bias=b1[:, m:m + 1]))
                    if e >= 1:
                        s_.wait_ge(c_t.h, tk[("cwrepps", t, e)])
                        if e >= 3:
                            s_.wait_ge(c_v.h, tk[("accdone", t, e - 2)])
                        assert tk[("cwrepe", t, e)] == c_s.inc(
                            s_.copy(f_cwrep[:, e % 2, :], p_st[:, 0:TF]))
                        tk[("cwrepe2", t, e)] = tk[("cwrepe", t, e)]
                    assert tk[("reludone", t, e)] == c_s.n
                block.scalar(f)


            def emit_w2(t, e, s):
                def f(t_, t=t, e=e, s=s):
                    hsC = (4 * e + 2) % 3
                    hsD = (4 * e + 3) % 3
                    t_.wait_ge(d_fs[hsC].h, fwb_tok[(t, e, 1, 0)])
                    t_.wait_ge(d_fs[hsD].h, fwb_tok[(t, e, 1, 1)])
                    t_.wait_ge(c_s.h, tk[("reludone", t, e)])
                    if e == 0:
                        if s <= 3:
                            t_.wait_ge(c_s.h, tk[("bcd", 2 * s + 1)])
                        else:
                            t_.wait_ge(c_s.h, tk[("uevac", 3, 1)])
                    is_shared = (e == 0)
                    is_last = (e == 15)
                    for dmg in range(4):
                        for j in range(2):
                            if e >= 1 or dmg >= 1:
                                prev = tk.get(("accvh", t, e, dmg - 1, j)) if dmg >= 1 \
                                    else tk.get(("accvh", t, e - 1, 3, j))
                                if prev is not None:
                                    t_.wait_ge(c_v.h, prev)
                            for k in range(NH):
                                stop = (k == NH - 1) and not (is_shared or is_last)
                                inst = t_.matmul(
                                    p_proj[:, 1024 + j * TF:1024 + (j + 1) * TF],
                                    f_wb[:, hsC if k < 4 else hsD, k % 4,
                                         (dmg * 2 + j) * 128:(dmg * 2 + j + 1) * 128],
                                    f_he[:, k, :], start=(k == 0), stop=stop)
                            m = dmg * 2 + j
                            if is_shared:
                                inst = t_.matmul(p_proj[:, 1024 + j * TF:1024 + (j + 1) * TF],
                                                 k_sb2b[0:1, m * 128:(m + 1) * 128],
                                                 k_ones_rb[0:1, 0:TF], start=False, stop=True)
                            elif is_last:
                                inst = t_.matmul(p_proj[:, 1024 + j * TF:1024 + (j + 1) * TF],
                                                 k_eb2b[:, m * 128:(m + 1) * 128],
                                                 f_cwTb[:], start=False, stop=True)
                            c_t.inc(inst)
                            tk[("w2ps", t, e, dmg, j)] = c_t.n
                    tk[("w2done", t, e)] = c_t.n
                    fwb_use[hsC] = c_t.n
                    fwb_use[hsD] = c_t.n
                block.tensor(f)

            def emit_acc(t, e):
                _vn = c_v.n

                def f(v, t=t, e=e, _vn=_vn):
                    v.wait_ge(c_v.h, _vn)
                    if e == 0 and t >= 1:
                        v.wait_ge(c_v.h, tk[("fout", t - 1, ND - 1)])
                    for dmg in range(4):
                        for j in range(2):
                            m = dmg * 2 + j
                            v.wait_ge(c_t.h, tk[("w2ps", t, e, dmg, j)])
                            if e == 0:
                                tok = c_v.inc(v.tensor_copy(
                                    f_acc[:, m, :], p_proj[:, 1024 + j * TF:1024 + (j + 1) * TF]))
                                assert tk[("accvh", t, e, dmg, j)] == tok
                            else:
                                bc = f_cwrep[:, e % 2, :]
                                tok = c_v.inc(v.tensor_mul(
                                    y_tmp[:], p_proj[:, 1024 + j * TF:1024 + (j + 1) * TF], bc))
                                assert tk[("accvh", t, e, dmg, j)] == tok
                                v.wait_ge(c_v.h, tok)
                                c_v.inc(v.tensor_add(f_acc[:, m, :], y_tmp[:], f_acc[:, m, :]))
                                v.wait_ge(c_v.h, c_v.n)
                    tk[("accdone", t, e)] = c_v.n
                block.vector(f)

            def emit_ffn_post(t, s):
                def f4_sq(s_, t=t):
                    s_.wait_ge(c_v.h, tk[("accdone", t, 15)])
                    for k in range(ND):
                        c_s.inc(s_.activation(f_sq[:, k, :], f_acc[:, k, :], ACT.Square))
                    tk[("sq4", t)] = c_s.n
                block.scalar(f4_sq)

                def f4_stats(t_, t=t):
                    t_.wait_ge(c_s.h, tk[("sq4", t)])
                    t_.wait_ge(c_s.h, tk[("cwrepe2", t, 15)])
                    t_.wait_ge(c_s.h, tk[("loge", t, 3)])
                    for k in range(ND):
                        inst = t_.matmul(p_st[0:1, 0:TF], k_ones_cb[:], f_acc[:, k, :],
                                         start=(k == 0), stop=(k == ND - 1))
                    c_t.inc(inst)
                    for k in range(ND):
                        inst = t_.matmul(p_dtr[0:1, 0:TF], k_ones_cb[:], f_sq[:, k, :],
                                         start=(k == 0), stop=(k == ND - 1))
                    tk[("fstat", "ln4", t)] = c_t.inc(inst)
                block.tensor(f4_stats)

                ffn_rows2("ln4", t)

                for k in range(ND):
                    co = c_o0 if k % 2 == 0 else c_o1
                    tk[("odma", t, k)] = co.n + 16 * (k // 2 + 1)

                _vn4 = c_v.n

                def f4_fin(v, t=t, _vn4=_vn4):
                    v.wait_ge(c_v.h, _vn4)
                    v.wait_ge(c_t.h, tk[("freps", "ln4", t)])
                    t1 = None
                    for k in range(ND):
                        prev = tk.get(("odma", t, k - 2)) if k >= 2 else \
                            tk.get(("odma", t - 1, ND - 2 + k))
                        if prev is not None:
                            v.wait_ge((c_o0 if k % 2 == 0 else c_o1).h, prev)
                        if t1 is not None:
                            v.wait_ge(c_v.h, t1)
                        t1 = c_v.inc(v.tensor_mul(y_tmp[:], f_acc[:, k, :], p_rep[:, 0:TF]))
                        v.wait_ge(c_v.h, t1)
                        t1 = c_v.inc(v.tensor_add(y_tmp[:], y_tmp[:], p_rep2[:, 0:TF]))
                        v.wait_ge(c_v.h, t1)
                        t1 = c_v.inc(v.tensor_add(f_out[:, k % 2, :], x2s[:, k, :], y_tmp[:]))
                        tk[("fout", t, k)] = t1
                block.vector(f4_fin)

                def f4_dma(g, t=t):
                    for k in range(ND):
                        co = c_o0 if k % 2 == 0 else c_o1
                        g.wait_ge(c_v.h, tk[("fout", t, k)])
                        g.dma_start(out_t[k * 128:(k + 1) * 128, t * TF:(t + 1) * TF],
                                    f_out[:, k % 2, :]).then_inc(co.h, 16)
                        co.n += 16
                        assert tk[("odma", t, k)] == co.n
                block.gpsimd(f4_dma)

            e_srcs = [(s_w1, s_w2, k_sb1, None)] + \
                     [(e_w1[e], e_w2[e], k_eb1[:, e, :], e) for e in range(NE)]
            EXPCAST = 16 * 2 * 8 * 16

            for c in range(NCH):
                sc_i = c // 2
                par = c % 2
                t0 = c * TC

                if c == 2:
                    def f_ecast(g):
                        for ei in range(16):
                            w1d, w2d = e_srcs[ei][0], e_srcs[ei][1]
                            for wi, wd in ((0, w1d), (1, w2d)):
                                for kt in range(8):
                                    g.dma_start(scr_e[ei, wi, kt],
                                                wd[kt * 128:(kt + 1) * 128, :]
                                                ).then_inc(d_ce.h, 16)
                                    d_ce.n += 16
                        assert d_ce.n == EXPCAST
                    block.gpsimd(f_ecast)

                def f_xdma(g, c=c, par=par, t0=t0):
                    g.wait_ge(d_gp.h, d_gp.n)
                    if c < 2:
                        g.wait_ge(d_sy.h, tk["scr_done"])
                    else:
                        g.wait_ge(c_v.h, tk[("x2done", (c - 2) // 2)])
                    for dtile in range(ND):
                        g.dma_start(xc[:, par, dtile, :],
                                    x_in[dtile * 128:(dtile + 1) * 128, t0:t0 + TC]
                                    ).then_inc(d_gp.h, 16)
                        d_gp.n += 16
                    tk[("xdma", c)] = d_gp.n
                block.gpsimd(f_xdma)

                def f_ln1sq(s, c=c, par=par):
                    s.wait_ge(d_gp.h, tk[("xdma", c)])
                    if stat_free is not None:
                        s.wait_ge(stat_free[0].h, stat_free[1])
                    if par == 0 and sc_i >= 2:
                        s.wait_ge(c_t.h, tk[("fstat", "ln4", sc_i - 2)])
                    for dtile in range(ND):
                        c_s.inc(s.activation(f_sq[:, dtile, 0:TC], xc[:, par, dtile, :],
                                             ACT.Square))
                    tk[("sq", "ln1", c)] = c_s.n
                block.scalar(f_ln1sq)

                ln_emit("ln1", c, [xc[:, par, k, :] for k in range(ND)],
                        [f_sq[:, k, 0:TC] for k in range(ND)], D,
                        (c_s, tk[("sq", "ln1", c)]))
                stat_free = (c_t, tk[("stat", "ln1", c)])

                def f_ln1n(v, c=c, par=par):
                    v.wait_ge(c_t.h, tk[("reps", "ln1", c)])
                    if c >= 1:
                        v.wait_ge(c_t.h, tk[("inproj", c - 1)])
                    for dtile in range(ND):
                        t1 = c_v.inc(v.tensor_mul(m_xln[:, dtile, :], xc[:, par, dtile, :],
                                                  p_rep[:, 0:TC]))
                        v.wait_ge(c_v.h, t1)
                        c_v.inc(v.tensor_add(m_xln[:, dtile, :], m_xln[:, dtile, :],
                                             p_rep2[:, 0:TC]))
                    tk[("xln", c)] = c_v.n
                block.vector(f_ln1n)
                rep_free = (c_v, tk[("xln", c)])

                for mg in range(8):
                    _bt = c_t.n
                    for k in range(8):
                        tk[("wku", c, mg, k)] = _bt + (k + 1 if k < 7 else 11)

                    def f_wdma(sy, c=c, mg=mg):
                        sy.wait_ge(d_sy.h, tk["scr_done"])
                        for k in range(8):
                            wpar = k % 2
                            dw = d_w0 if wpar == 0 else d_w1
                            if k < 2:
                                if wkt_used[wpar] is not None:
                                    sy.wait_ge(c_t.h, wkt_used[wpar])
                            else:
                                sy.wait_ge(c_t.h, tk[("wku", c, mg, k - 2)])
                            sy.dma_start(w_kt[:, wpar, :], scr_in[mg, k]).then_inc(dw.h, 16)
                            dw.n += 16
                            tk[("wk", c, mg, k)] = dw.n
                    block.sync(f_wdma)

                    def f_mm(t, c=c, mg=mg, par=par, sc_i=sc_i):
                        t.wait_ge(c_v.h, tk[("xln", c)])
                        if mg >= 1:
                            t.wait_ge(c_s.h, tk[("evac", c, mg - 1)])
                        elif c >= 1:
                            t.wait_ge(c_s.h, tk[("evac", c - 1, 7)])
                            t.wait_ge(c_s.h, tk[("bcd", c - 1)])
                        if mg == 0 and par == 0 and sc_i >= 1:
                            t.wait_ge(c_s.h, tk[("uevac", sc_i - 1, 1)])
                        for k in range(8):
                            wpar = k % 2
                            dw = d_w0 if wpar == 0 else d_w1
                            t.wait_ge(dw.h, tk[("wk", c, mg, k)])
                            for j in range(4):
                                inst = t.matmul(p_proj[:, j * 512:j * 512 + TC],
                                                w_kt[:, wpar, j * 128:(j + 1) * 128],
                                                m_xln[:, k, :], start=(k == 0), stop=(k == 7))
                                if k == 7:
                                    c_t.inc(inst)
                            if k < 7:
                                assert tk[("wku", c, mg, k)] == c_t.inc(inst)
                            else:
                                assert tk[("wku", c, mg, k)] == c_t.n
                        wkt_used[0] = tk[("wku", c, mg, 6)]
                        wkt_used[1] = tk[("wku", c, mg, 7)]
                        tk[("ippsum", c, mg)] = c_t.n
                    block.tensor(f_mm)

                    def f_evac(s, c=c, mg=mg, par=par, sc_i=sc_i):
                        s.wait_ge(c_t.h, tk[("ippsum", c, mg)])
                        if mg < 4:
                            if c >= 1:
                                s.wait_ge(c_v.h, tk[("convdone", c - 1)])
                            if c >= 2:
                                s.wait_ge(c_v.h, tk[("x2done", sc_i - 1)])
                        else:
                            if par == 0 and sc_i >= 1:
                                s.wait_ge(c_t.h, tk[("uproj", sc_i - 1)])
                        for j in range(4):
                            m = mg * 4 + j
                            if mg < 4:
                                c_s.inc(s.copy(xp(m, 0, TC), p_proj[:, j * 512:j * 512 + TC]))
                            else:
                                c_s.inc(s.activation(z_s[:, m - 16, par * TC:(par + 1) * TC],
                                                     p_proj[:, j * 512:j * 512 + TC], ACT.Silu))
                        tk[("evac", c, mg)] = c_s.n
                    block.scalar(f_evac)
                tk[("inproj", c)] = c_t.n

                if "xpre0" in taps and c == 0:
                    def f_tap_xpre(g):
                        g.wait_ge(c_s.h, tk[("evac", 0, 3)])
                        g.wait_ge(d_gp.h, d_gp.n)
                        g.dma_start(taps["xpre0"], u_pre[:]).then_inc(d_gp.h, 16)
                        d_gp.n += 16
                    block.gpsimd(f_tap_xpre)

                for g_ in range(NG):
                    tk[("silu", c, g_)] = c_s.n + g_ + 1
                tk[("siludone", c)] = c_s.n + NG

                def f_conv(v, c=c):
                    v.wait_ge(c_s.h, tk[("evac", c, 3)])
                    if c >= 1:
                        v.wait_ge(c_s.h, tk[("siludone", c - 1)])
                    for g_ in range(NG):
                        cp = g_ % 2
                        if g_ >= 2:
                            v.wait_ge(c_s.h, tk[("silu", c, g_ - 2)])
                        t1 = c_v.inc(v.tensor_scalar(conv_acc[:, cp, :], xp(g_, 0, TC),
                                                     k_conv_w[:, g_, 3:4], None, AL.mult))
                        for kk in range(3):
                            off = 3 - kk
                            v.wait_ge(c_v.h, t1)
                            t1 = c_v.inc(v.scalar_tensor_tensor(
                                conv_acc[:, cp, off:TC], xp(g_, 0, TC - off),
                                k_conv_w[:, g_, kk:kk + 1],
                                conv_acc[:, cp, off:TC], AL.mult, AL.add))
                            if c > 0:
                                v.wait_ge(c_v.h, t1)
                                t1 = c_v.inc(v.scalar_tensor_tensor(
                                    conv_acc[:, cp, 0:off], x_carry[:, g_, 4 - off:4],
                                    k_conv_w[:, g_, kk:kk + 1],
                                    conv_acc[:, cp, 0:off], AL.mult, AL.add))
                        tk[("conv", c, g_)] = t1
                    v.wait_ge(c_v.h, tk[("conv", c, NG - 1)])
                    tk[("convdone", c)] = c_v.inc(
                        v.tensor_copy(x_carry[:, :, 1:4], u_pre[:].rearrange('p (g t) -> p g t', g=NG)[:, :, TC - 3:TC]))
                block.vector(f_conv)

                def f_silu(s, c=c, par=par, sc_i=sc_i):
                    if par == 0 and sc_i >= 1:
                        s.wait_ge(c_v.h, tk[("ygdone", sc_i - 1)])
                    for g_ in range(NG):
                        cp = g_ % 2
                        s.wait_ge(c_v.h, tk[("conv", c, g_)])
                        inst = s.activation(x_post[:, g_, par * TC:(par + 1) * TC],
                                            conv_acc[:, cp, :], ACT.Silu,
                                            bias=k_conv_b[:, g_:g_ + 1])
                        assert tk[("silu", c, g_)] == c_s.inc(inst)
                    assert tk[("siludone", c)] == c_s.n
                block.scalar(f_silu)

                def f_xproj(t, c=c, par=par, sc_i=sc_i):
                    t.wait_ge(c_s.h, tk[("siludone", c)])
                    if c >= 1:
                        t.wait_ge(c_s.h, tk[("bcd", c - 1)])
                    if par == 0 and sc_i >= 1:
                        t.wait_ge(c_s.h, tk[("uevac", sc_i - 1, 1)])
                    for k in range(NG):
                        inst = t.matmul(p_proj[0:33, 1792:2048],
                                        k_xp_b[:, k, :], x_post[:, k, par * TC:(par + 1) * TC],
                                        start=(k == 0), stop=(k == NG - 1))
                    tk[("bcdps", c)] = c_t.inc(inst)
                block.tensor(f_xproj)

                def f_bcdevac(s, c=c, par=par, sc_i=sc_i):
                    s.wait_ge(c_t.h, tk[("bcdps", c)])
                    if sc_i >= 1 and par == 0:
                        s.wait_ge(c_t.h, tk[("creps", sc_i - 1)])
                    tk[("bcd", c)] = c_s.inc(
                        s.copy(bcdT[:, par, :], p_proj[0:33, 1792:2048]))
                block.scalar(f_bcdevac)

                if par == 0:
                    continue

                st0 = sc_i * SC

                if "bcd0" in taps and sc_i == 0:
                    def f_tap_bcd(g):
                        g.wait_ge(c_s.h, tk[("bcd", 1)])
                        g.wait_ge(d_gp.h, d_gp.n)
                        g.dma_start(taps["bcd0"], bcdT[:]).then_inc(d_gp.h, 16)
                        d_gp.n += 16
                    block.gpsimd(f_tap_bcd)

                def f_dtr(t, sc_i=sc_i, c=c):
                    t.wait_ge(c_s.h, tk[("bcd", c)])
                    if sc_i >= 1:
                        t.wait_ge(c_s.h, tk[("dtrev", sc_i - 1)])
                        t.wait_ge(c_s.h, tk[("dtdone", sc_i - 1)])
                    if sc_i >= 2:
                        t.wait_ge(c_v.h, tk[("fvar", "ln4", sc_i - 2)])
                    for p2 in range(2):
                        c_t.inc(t.matmul(p_dtr[:, p2 * TC:(p2 + 1) * TC],
                                         k_sel[:, 32 * 128:33 * 128],
                                         bcdT[:, p2, :], start=True, stop=True))
                    tk[("dtr", sc_i)] = c_t.n
                block.tensor(f_dtr)

                def f_dtrevac(s, sc_i=sc_i):
                    s.wait_ge(c_t.h, tk[("dtr", sc_i)])
                    tk[("dtrev", sc_i)] = c_s.inc(s.copy(m_dtr[:], p_dtr[:, 0:SC]))
                block.scalar(f_dtrevac)

                _bsev = c_s.n
                for n in range(NS):
                    tk[("brep", sc_i, n)] = _bsev + n + 1

                def f_breps(t, sc_i=sc_i):
                    nonlocal rep_free
                    if rep_free is not None:
                        t.wait_ge(rep_free[0].h, rep_free[1])
                    if sc_i >= 1:
                        t.wait_ge(c_v.h, tk[("scandone", sc_i - 1)])
                    for n in range(NS):
                        if n >= 1:
                            t.wait_ge(c_s.h, tk[("brep", sc_i, n - 1)])
                        for p2 in range(2):
                            c_t.inc(t.matmul(p_rep[:, p2 * TC:(p2 + 1) * TC],
                                             k_sel[:, n * 128:(n + 1) * 128],
                                             bcdT[:, p2, :], start=True, stop=True))
                        tk[("brp", sc_i, n)] = c_t.n
                block.tensor(f_breps)

                def f_brevac(s, sc_i=sc_i):
                    for n in range(NS):
                        s.wait_ge(c_t.h, tk[("brp", sc_i, n)])
                        assert tk[("brep", sc_i, n)] == c_s.inc(
                            s.copy(brep[:, n, :], p_rep[:, 0:SC]))
                block.scalar(f_brevac)

                _csev = c_s.n
                for n in range(NS):
                    tk[("crep", sc_i, n)] = _csev + n + 1

                def f_creps(t, sc_i=sc_i):
                    t.wait_ge(c_s.h, tk[("brep", sc_i, NS - 1)])
                    if sc_i >= 1:
                        t.wait_ge(c_v.h, tk[("ygdone", sc_i - 1)])
                    for n in range(NS):
                        if n >= 1:
                            t.wait_ge(c_s.h, tk[("crep", sc_i, n - 1)])
                        for p2 in range(2):
                            c_t.inc(t.matmul(p_rep[:, p2 * TC:(p2 + 1) * TC],
                                             k_sel[:, (16 + n) * 128:(17 + n) * 128],
                                             bcdT[:, p2, :], start=True, stop=True))
                        tk[("crp", sc_i, n)] = c_t.n
                    tk[("creps", sc_i)] = c_t.n
                block.tensor(f_creps)

                def f_crevac(s, sc_i=sc_i):
                    for n in range(NS):
                        s.wait_ge(c_t.h, tk[("crp", sc_i, n)])
                        assert tk[("crep", sc_i, n)] == c_s.inc(
                            s.copy(crep[:, n, :], p_rep[:, 0:SC]))
                block.scalar(f_crevac)
                rep_free = (c_s, tk[("crep", sc_i, NS - 1)])

                _t = sc_i - 1 if sc_i >= 1 else None
                if _t is not None:
                    emit_ffn_pre(_t, sc_i)
                    relu_layout(_t, c_s.n, True)

                def acc_tokens(t_, e_, pos):
                    if e_ == 0:
                        for dmg in range(4):
                            for jj in range(2):
                                tk[("accvh", t_, e_, dmg, jj)] = pos + 2 * dmg + jj + 1
                        return pos + 8
                    for dmg in range(4):
                        for jj in range(2):
                            tk[("accvh", t_, e_, dmg, jj)] = pos + 2 * (2 * dmg + jj) + 1
                    return pos + 16

                _pos = c_v.n
                for k_ in range(NG):
                    if _t is not None and k_ >= 1:
                        _pos = acc_tokens(_t, k_ - 1, _pos)
                    tk[("wdt", sc_i, k_)] = _pos + 1
                    for pp in range(NS // 2):
                        tk[("dbx", sc_i, k_ * NS + 2 * pp)] = _pos + 2 + 3 * pp
                        tk[("scan", sc_i, k_ * NS + 2 * pp)] = _pos + 3 + 3 * pp
                        tk[("scan", sc_i, k_ * NS + 2 * pp + 1)] = _pos + 4 + 3 * pp
                    tk[("hcp", sc_i, k_)] = _pos + 26
                    tk[("yg", sc_i, k_)] = _pos + 37
                    _pos += 37
                if _t is not None:
                    _pos = acc_tokens(_t, NG - 1, _pos)
                tk[("scandone", sc_i)] = tk[("hcp", sc_i, NG - 1)]
                tk[("ygdone", sc_i)] = tk[("yg", sc_i, NG - 1)]

                def mk_dtda_g(g_, sc_i=sc_i):
                    def f(s, sc_i=sc_i, g_=g_):
                        if g_ == 0:
                            s.wait_ge(c_s.h, tk[("dtrev", sc_i)])
                        if g_ >= 1:
                            s.wait_ge(c_v.h, tk[("wdt", sc_i, g_ - 1)])
                            s.wait_ge(c_s.h, tk[("da", sc_i, g_ * NS - 1)])
                        t1 = c_s.inc(s.activation(m_dtg[:], m_dtr[:], ACT.Exp,
                                                  scale=k_dtw[:, g_:g_ + 1],
                                                  bias=k_dtb[:, g_:g_ + 1]))
                        s.wait_ge(c_s.h, t1)
                        tk[("dt", sc_i, g_)] = c_s.inc(
                            s.activation(m_dtg[:], m_dtg[:], ACT.Ln, bias=1.0))
                        s.wait_ge(c_s.h, tk[("dt", sc_i, g_)])
                        for n in range(NS):
                            i = g_ * NS + n
                            dpar = i % 2
                            prev = tk.get(("scan", sc_i, i - 2)) if i >= 2 else \
                                (tk.get(("scan", sc_i - 1, 254 + i)) if sc_i >= 1 else None)
                            if prev is not None:
                                s.wait_ge(c_v.h, prev)
                            tk[("da", sc_i, i)] = c_s.inc(
                                s.activation(m_da[:, dpar, :], m_dtg[:], ACT.Exp,
                                             scale=k_a[:, g_, n:n + 1]))
                        if g_ == NG - 1:
                            tk[("dtdone", sc_i)] = c_s.n
                    block.scalar(f)

                def mk_scan_g(g_, sc_i=sc_i):
                    def f(v, sc_i=sc_i, g_=g_):
                        if g_ == 0:
                            v.wait_ge(c_s.h, tk[("evac", 2 * sc_i + 1, 7)])
                            v.wait_ge(c_s.h, tk[("siludone", 2 * sc_i + 1)])
                            if sc_i >= 1:
                                v.wait_ge(c_t.h, tk[("uproj", sc_i - 1)])
                            v.wait_ge(c_s.h, tk[("brep", sc_i, NS - 1)])
                        v.wait_ge(c_s.h, tk[("dt", sc_i, g_)])
                        assert tk[("wdt", sc_i, g_)] == c_v.inc(
                            v.tensor_mul(m_wdt[:], m_dtg[:], x_post[:, g_, :]))
                        for n in range(NS):
                            i = g_ * NS + n
                            dpar = i % 2
                            v.wait_ge(c_s.h, tk[("da", sc_i, i)])
                            if n % 2 == 0:
                                if n == 0:
                                    v.wait_ge(c_v.h, tk[("wdt", sc_i, g_)])
                                else:
                                    v.wait_ge(c_v.h, tk[("scan", sc_i, i - 1)])
                                if g_ == 0 and n == 0:
                                    v.wait_ge(c_s.h, tk[("crep", sc_i, NS - 1)])
                                bcast = m_wdt[:].unsqueeze(1).broadcast_to((128, 2, SC))
                                assert tk[("dbx", sc_i, i)] == c_v.inc(
                                    v.tensor_mul(m_dbx[:], bcast, brep[:, n:n + 2, :]))
                                v.wait_ge(c_v.h, tk[("dbx", sc_i, i)])
                            else:
                                v.wait_ge(c_v.h, tk[("scan", sc_i, i - 1)])
                            init = 0.0 if sc_i == 0 else h_last[:, g_, n:n + 1]
                            assert tk[("scan", sc_i, i)] == c_v.inc(
                                v.tensor_tensor_scan(h_buf[:, n, :], m_da[:, dpar, :],
                                                     m_dbx[:, n % 2, :], init,
                                                     AL.mult, AL.add))
                        v.wait_ge(c_v.h, tk[("scan", sc_i, g_ * NS + NS - 1)])
                        assert tk[("hcp", sc_i, g_)] == c_v.inc(
                            v.tensor_copy(h_last[:, g_, :], h_buf[:, :, SC - 1]))
                        v.wait_ge(c_v.h, tk[("hcp", sc_i, g_)])
                        for q in range(4):
                            t1 = c_v.inc(v.tensor_mul(h_buf[:, 4 * q:4 * q + 4, :],
                                                      h_buf[:, 4 * q:4 * q + 4, :],
                                                      crep[:, 4 * q:4 * q + 4, :]))
                        v.wait_ge(c_v.h, t1)
                        c_v.inc(v.tensor_add(h_buf[:, 0:4, :], h_buf[:, 0:4, :],
                                             h_buf[:, 8:12, :]))
                        t1 = c_v.inc(v.tensor_add(h_buf[:, 4:8, :], h_buf[:, 4:8, :],
                                                  h_buf[:, 12:16, :]))
                        v.wait_ge(c_v.h, t1)
                        t1 = c_v.inc(v.tensor_add(h_buf[:, 0:4, :], h_buf[:, 0:4, :],
                                                  h_buf[:, 4:8, :]))
                        v.wait_ge(c_v.h, t1)
                        t1 = c_v.inc(v.tensor_add(h_buf[:, 0:2, :], h_buf[:, 0:2, :],
                                                  h_buf[:, 2:4, :]))
                        v.wait_ge(c_v.h, t1)
                        t1 = c_v.inc(v.tensor_add(h_buf[:, 0, :], h_buf[:, 0, :],
                                                  h_buf[:, 1, :]))
                        v.wait_ge(c_v.h, t1)
                        t1 = c_v.inc(v.scalar_tensor_tensor(y_tmp[:], x_post[:, g_, :],
                                                            k_d[:, g_:g_ + 1],
                                                            h_buf[:, 0, :], AL.mult, AL.add))
                        v.wait_ge(c_v.h, t1)
                        assert tk[("yg", sc_i, g_)] == c_v.inc(
                            v.tensor_mul(y_g[:, g_, :], y_tmp[:], z_s[:, g_, :]))
                    block.vector(f)

                for k_ in range(NG):
                    if _t is not None and k_ >= 1:
                        emit_relu(_t, k_ - 1)
                    mk_dtda_g(k_)
                    if _t is not None:
                        emit_cwrep(_t, k_)
                        if k_ >= 1:
                            emit_w2(_t, k_ - 1, sc_i)
                        emit_fwa(_t, k_)
                        emit_w1(_t, k_, sc_i)
                        emit_fwb(_t, k_)
                    if _t is not None and k_ >= 1:
                        emit_acc(_t, k_ - 1)
                    mk_scan_g(k_)

                if _t is not None:
                    emit_relu(_t, NG - 1)
                    emit_w2(_t, NG - 1, sc_i)
                    emit_acc(_t, NG - 1)
                    emit_ffn_post(_t, sc_i)
                    stat_free = (c_v, tk[("fvar", "ln4", _t)])
                    rep_free = (c_v, tk[("fout", _t, ND - 1)])

                if "xpost0" in taps and sc_i == 0:
                    def f_tap_xpost(g):
                        g.wait_ge(c_s.h, tk[("siludone", 1)])
                        g.wait_ge(d_gp.h, d_gp.n)
                        g.dma_start(taps["xpost0"], x_post[:]).then_inc(d_gp.h, 16)
                        d_gp.n += 16
                    block.gpsimd(f_tap_xpost)
                if "yg0" in taps and sc_i == 0:
                    def f_tap_yg(g):
                        g.wait_ge(c_v.h, tk[("ygdone", 0)])
                        g.wait_ge(d_gp.h, d_gp.n)
                        g.dma_start(taps["yg0"], y_g[:]).then_inc(d_gp.h, 16)
                        d_gp.n += 16
                    block.gpsimd(f_tap_yg)

                for half in range(2):
                    _bt = c_t.n
                    for k in range(NG):
                        tk[("oku", sc_i, half, k)] = _bt + (k + 1 if k < NG - 1 else NG + 3)

                    def f_odma(sy, sc_i=sc_i, half=half):
                        for k in range(NG):
                            wpar = k % 2
                            dw = d_w0 if wpar == 0 else d_w1
                            if k < 2:
                                if wkt_used[wpar] is not None:
                                    sy.wait_ge(c_t.h, wkt_used[wpar])
                            else:
                                sy.wait_ge(c_t.h, tk[("oku", sc_i, half, k - 2)])
                            sy.dma_start(w_kt[:, wpar, :], scr_out[half, k]).then_inc(dw.h, 16)
                            dw.n += 16
                            tk[("ok", sc_i, half, k)] = dw.n
                    block.sync(f_odma)

                    def f_omm(t, sc_i=sc_i, half=half):
                        if half == 0:
                            t.wait_ge(c_s.h, tk[("bcd", 2 * sc_i + 1)])
                            t.wait_ge(c_s.h, tk[("evac", 2 * sc_i + 1, 7)])
                            if sc_i >= 1:
                                t.wait_ge(c_s.h, tk[("reludone", sc_i - 1, NG - 1)])
                                t.wait_ge(c_v.h, tk[("accdone", sc_i - 1, 15)])
                        else:
                            t.wait_ge(c_s.h, tk[("uevac", sc_i, 0)])
                        for k in range(NG):
                            wpar = k % 2
                            dw = d_w0 if wpar == 0 else d_w1
                            t.wait_ge(dw.h, tk[("ok", sc_i, half, k)])
                            t.wait_ge(c_v.h, tk[("yg", sc_i, k)])
                            for j in range(4):
                                inst = t.matmul(p_proj[:, j * SC:(j + 1) * SC],
                                                w_kt[:, wpar, j * 128:(j + 1) * 128],
                                                y_g[:, k, :], start=(k == 0), stop=(k == NG - 1))
                                if k == NG - 1:
                                    c_t.inc(inst)
                            if k < NG - 1:
                                assert tk[("oku", sc_i, half, k)] == c_t.inc(inst)
                            else:
                                assert tk[("oku", sc_i, half, k)] == c_t.n
                        wkt_used[0] = tk[("oku", sc_i, half, NG - 2)]
                        wkt_used[1] = tk[("oku", sc_i, half, NG - 1)]
                        tk[("ups", sc_i, half)] = c_t.n
                    block.tensor(f_omm)

                    def f_uevac(s, sc_i=sc_i, half=half):
                        s.wait_ge(c_t.h, tk[("ups", sc_i, half)])
                        s.wait_ge(c_v.h, tk[("convdone", 2 * sc_i + 1)])
                        if sc_i >= 1 and half == 0:
                            s.wait_ge(c_v.h, tk[("x2done", sc_i - 1)])
                        for j in range(4):
                            c_s.inc(s.copy(mu(half * 4 + j, 0, SC),
                                           p_proj[:, j * SC:(j + 1) * SC]))
                        tk[("uevac", sc_i, half)] = c_s.n
                    block.scalar(f_uevac)
                tk[("uproj", sc_i)] = c_t.n

                if "u0" in taps and sc_i == 0:
                    def f_tap_u(g):
                        g.wait_ge(c_s.h, tk[("uevac", 0, 1)])
                        g.wait_ge(d_gp.h, d_gp.n)
                        g.dma_start(taps["u0"], u_pre[:]).then_inc(d_gp.h, 16)
                        d_gp.n += 16
                    block.gpsimd(f_tap_u)

                for p2 in range(2):
                    cc = 2 * sc_i + p2

                    def f_usq(s, sc_i=sc_i, p2=p2, cc=cc):
                        s.wait_ge(c_s.h, tk[("uevac", sc_i, 1)])
                        if sc_i >= 1:
                            s.wait_ge(c_t.h, tk[("fstat", "ln4", sc_i - 1)])
                        if stat_free is not None:
                            s.wait_ge(stat_free[0].h, stat_free[1])
                        for dtile in range(ND):
                            c_s.inc(s.activation(f_sq[:, dtile, 0:TC],
                                                 mu(dtile, p2 * TC, (p2 + 1) * TC),
                                                 ACT.Square))
                        tk[("sq", "ln2", cc)] = c_s.n
                    block.scalar(f_usq)

                    ln_emit("ln2", cc,
                            [mu(k, p2 * TC, (p2 + 1) * TC) for k in range(ND)],
                            [f_sq[:, k, 0:TC] for k in range(ND)], D,
                            (c_s, tk[("sq", "ln2", cc)]))
                    stat_free = (c_t, tk[("stat", "ln2", cc)])

                    def f_x2(v, sc_i=sc_i, p2=p2, cc=cc, st0=st0):
                        v.wait_ge(c_t.h, tk[("reps", "ln2", cc)])
                        if sc_i >= 1:
                            v.wait_ge(c_v.h, tk[("fout", sc_i - 1, ND - 1)])
                        t1 = None
                        for dtile in range(ND):
                            if t1 is not None:
                                v.wait_ge(c_v.h, t1)
                            t1 = c_v.inc(v.tensor_mul(y_tmp[:, 0:TC],
                                                      mu(dtile, p2 * TC, (p2 + 1) * TC),
                                                      p_rep[:, 0:TC]))
                            v.wait_ge(c_v.h, t1)
                            t1 = c_v.inc(v.tensor_add(y_tmp[:, 0:TC], y_tmp[:, 0:TC],
                                                      p_rep2[:, 0:TC]))
                            v.wait_ge(c_v.h, t1)
                            t1 = c_v.inc(v.tensor_add(x2s[:, dtile, p2 * TC:(p2 + 1) * TC],
                                                      xc[:, p2, dtile, :], y_tmp[:, 0:TC]))
                        tk[("x2h", sc_i, p2)] = c_v.n
                    block.vector(f_x2)
                    rep_free = (c_v, tk[("x2h", sc_i, p2)])
                tk[("x2done", sc_i)] = c_v.n

            _t = 3
            emit_ffn_pre(_t, 4)
            relu_layout(_t, c_s.n, False)
            _pos = c_v.n
            def acc_tokens_t(t_, e_, pos):
                if e_ == 0:
                    for dmg in range(4):
                        for jj in range(2):
                            tk[("accvh", t_, e_, dmg, jj)] = pos + 2 * dmg + jj + 1
                    return pos + 8
                for dmg in range(4):
                    for jj in range(2):
                        tk[("accvh", t_, e_, dmg, jj)] = pos + 2 * (2 * dmg + jj) + 1
                return pos + 16
            for e_ in range(NG):
                _pos = acc_tokens_t(_t, e_, _pos)
            for k_ in range(NG):
                if k_ >= 1:
                    emit_relu(_t, k_ - 1)
                emit_cwrep(_t, k_)
                if k_ >= 1:
                    emit_w2(_t, k_ - 1, 4)
                emit_fwa(_t, k_)
                emit_w1(_t, k_, 4)
                emit_fwb(_t, k_)
                if k_ >= 1:
                    emit_acc(_t, k_ - 1)
            emit_relu(_t, NG - 1)
            emit_w2(_t, NG - 1, 4)
            emit_acc(_t, NG - 1)
            emit_ffn_post(_t, 4)

            def f_final(sy):
                sy.wait_ge(c_o0.h, c_o0.n)
                sy.wait_ge(c_o1.h, c_o1.n)
            block.sync(f_final)

            def f_final_g(g):
                g.wait_ge(d_gp.h, d_gp.n)
                g.wait_ge(d_ce.h, d_ce.n)
            block.gpsimd(f_final_g)

    return nc


def _layout_inputs(inputs):
    import ml_dtypes
    g = {}
    g["w_in"] = np.ascontiguousarray(inputs["L0_attn_in_proj_w"])
    cw = np.asarray(inputs["L0_attn_conv_w"]).reshape(DI, 4)
    g["w_conv"] = np.ascontiguousarray(cw.reshape(NG, 128, 4).transpose(1, 0, 2))
    g["conv_b"] = np.ascontiguousarray(np.asarray(inputs["L0_attn_conv_b"]).reshape(NG, 128).T)
    g["w_xp"] = np.ascontiguousarray(
        np.asarray(inputs["L0_attn_x_proj_w"]).reshape(NG, 128, 33).transpose(1, 0, 2))
    g["dt_w"] = np.ascontiguousarray(np.asarray(inputs["L0_attn_dt_w"]).reshape(NG, 128).T)
    g["dt_b"] = np.ascontiguousarray(np.asarray(inputs["L0_attn_dt_b"]).reshape(NG, 128).T)
    g["a_log"] = np.ascontiguousarray(
        np.asarray(inputs["L0_attn_A_log"]).reshape(NG, 128, NS).transpose(1, 0, 2))
    g["d_par"] = np.ascontiguousarray(np.asarray(inputs["L0_attn_D"]).reshape(NG, 128).T)
    g["w_out"] = np.ascontiguousarray(inputs["L0_attn_out_proj_w"])
    g["r_w"] = np.ascontiguousarray(
        np.asarray(inputs["L0_ffn_router_W"]).reshape(ND, 128, NE).transpose(1, 0, 2))
    g["r_b"] = np.ascontiguousarray(np.asarray(inputs["L0_ffn_router_b"]).reshape(1, NE))
    g["s_w1"] = np.ascontiguousarray(inputs["L0_ffn_sW1"])
    g["s_b1"] = np.ascontiguousarray(np.asarray(inputs["L0_ffn_sb1"]).reshape(NH, 128).T)
    g["s_w2"] = np.ascontiguousarray(inputs["L0_ffn_sW2"])
    g["s_b2"] = np.ascontiguousarray(np.asarray(inputs["L0_ffn_sb2"]).reshape(1, D))
    g["e_w1"] = np.ascontiguousarray(inputs["L0_ffn_eW1"])
    g["e_b1"] = np.ascontiguousarray(
        np.asarray(inputs["L0_ffn_eb1"]).T.reshape(NH, 128, NE).transpose(1, 2, 0))
    g["e_w2"] = np.ascontiguousarray(inputs["L0_ffn_eW2"])
    g["e_b2"] = np.ascontiguousarray(inputs["L0_ffn_eb2"])
    g["cst"] = np.tile(np.array([[EPS, 1.0, -0.5, 0.0]], np.float32), (128, 1))
    g["ones_r"] = np.ones((1, TF), np.float32)
    g["ones_rb"] = np.ones((1, TF), ml_dtypes.bfloat16)
    g["ones_cb"] = np.ones((128, 1), ml_dtypes.bfloat16)
    g["eye"] = np.eye(128, dtype=np.float32)
    selm = np.zeros((33, 33 * 128), np.float32)
    for n in range(33):
        selm[n, n * 128:(n + 1) * 128] = 1.0
    g["sel"] = selm.astype(ml_dtypes.bfloat16)
    return g


def kernel(**inputs):
    g = _layout_inputs(inputs)
    x = np.asarray(inputs["L0_attn_x"])
    nc = build_program()
    in_maps = []
    for b in range(8):
        m = dict(g)
        m["x"] = np.ascontiguousarray(x[b].T)
        in_maps.append(m)
    res = run_bass_kernel_spmd(nc, in_maps, core_ids=list(range(8)))
    out = np.stack([np.asarray(r["out"]).T for r in res.results], axis=0)
    return np.ascontiguousarray(out.astype(np.float32))


# version 10 (final)
# speedup vs baseline: 1.0328x; 1.0328x over previous
import contextlib
import numpy as np
import concourse.bass as bass
import concourse.mybir as mybir
from concourse.bass_utils import run_bass_kernel_spmd

F32 = mybir.dt.float32
F16 = mybir.dt.float16
BF16 = mybir.dt.bfloat16
AL = mybir.AluOpType
ACT = mybir.ActivationFunctionType
AX = mybir.AxisListType

D = 1024
DI = 2048
L = 2048
NS = 16
NE = 15
HF = 1024
ND = D // 128
NG = DI // 128
NH = HF // 128
TC = 256
NCH = L // TC
SC = 512
NSC = L // SC
TF = 512
NTF = L // TF
EPS = 1e-5

_DEBUG_TAPS = []
_STOP = [99]


class Tok:
    def __init__(self, handle):
        self.h = handle
        self.n = 0

    def inc(self, inst, k=1):
        inst.then_inc(self.h, k)
        self.n += k
        return self.n


def build_program():
    nc = bass.Bass("TRN2", target_bir_lowering=False)

    def din(name, shape, dt=F32):
        return nc.dram_tensor(name, list(shape), dt, kind="ExternalInput").ap()

    x_in = din("x", (D, L))
    w_in = din("w_in", (D, 2 * DI))
    w_conv = din("w_conv", (128, NG, 4))
    conv_b = din("conv_b", (128, NG))
    w_xp = din("w_xp", (128, NG, 33))
    dt_w = din("dt_w", (128, NG))
    dt_b = din("dt_b", (128, NG))
    a_log = din("a_log", (128, NG, NS))
    d_par = din("d_par", (128, NG))
    w_out = din("w_out", (DI, D))
    r_w = din("r_w", (128, ND, NE))
    r_b = din("r_b", (1, NE))
    s_w1 = din("s_w1", (D, HF))
    s_b1 = din("s_b1", (128, NH))
    s_w2 = din("s_w2", (HF, D))
    s_b2 = din("s_b2", (1, D))
    e_w1 = din("e_w1", (NE, D, HF))
    e_b1 = din("e_b1", (128, NE, NH))
    e_w2 = din("e_w2", (NE, HF, D))
    e_b2 = din("e_b2", (NE, D))
    cst = din("cst", (128, 4))
    ones_r = din("ones_r", (1, TF))
    ones_rb = din("ones_rb", (1, TF), BF16)
    ones_cb = din("ones_cb", (128, 1), BF16)
    eye = din("eye", (128, 128))
    sel = din("sel", (33, 33 * 128), BF16)

    out_t = nc.dram_tensor("out", [D, L], F32, kind="ExternalOutput").ap()
    taps = {}
    tap_shapes = {
        "xpre0": [128, NG, TC], "xpost0": [128, NG, SC],
        "bcd0": [33, 2, TC], "yg0": [128, NG, SC], "x2": [128, ND, L],
        "cw0": [128, NE], "u0": [128, ND, SC],
    }
    for tap in _DEBUG_TAPS:
        taps[tap] = nc.dram_tensor("tap_" + tap, tap_shapes[tap], F32,
                                   kind="ExternalOutput").ap()

    scr_in = nc.dram_tensor("scr_in", [8, 8, 128, 512], BF16, kind="Internal").ap()
    scr_out = nc.dram_tensor("scr_out", [2, 16, 128, 512], BF16, kind="Internal").ap()
    scr_e = nc.dram_tensor("scr_e", [16, 2, 8, 128, 1024], BF16, kind="Internal").ap()

    st = contextlib.ExitStack()
    with st:
        def sb(name, shape, dt=F32):
            return st.enter_context(nc.sbuf_tensor(name, list(shape), dt))

        def sem(name):
            return Tok(st.enter_context(nc.semaphore(name)))

        d_sy = sem("d_sy")
        d_gp = sem("d_gp")
        c_t = sem("c_t")
        c_s = sem("c_s")
        c_v = sem("c_v")
        c_g = sem("c_g")
        c_o0 = sem("c_o0")
        c_o1 = sem("c_o1")
        d_w0 = sem("d_w0")
        d_w1 = sem("d_w1")
        d_w2 = sem("d_w2")
        d_w3 = sem("d_w3")
        d_ce = sem("d_ce")
        d_f0 = sem("d_f0")
        d_f1 = sem("d_f1")
        d_f2 = sem("d_f2")

        k_conv_w = sb("k_conv_w", (128, NG, 4))
        k_conv_b = sb("k_conv_b", (128, NG))
        k_dtw = sb("k_dtw", (128, NG))
        k_dtb = sb("k_dtb", (128, NG))
        k_a = sb("k_a", (128, NG, NS))
        k_d = sb("k_d", (128, NG))
        k_sb1 = sb("k_sb1", (128, NH))
        k_eb1 = sb("k_eb1", (128, NE, NH))
        k_sb2b = sb("k_sb2b", (1, D), BF16)
        k_eb2b = sb("k_eb2b", (NE, D), BF16)
        k_rb_b = sb("k_rb_b", (1, NE), BF16)
        k_rw_b = sb("k_rw_b", (128, ND, NE), BF16)
        k_xp_b = sb("k_xp_b", (128, NG, 33), BF16)
        k_ones_r = sb("k_ones_r", (1, 128))
        k_ones_h = sb("k_ones_h", (1, 128), F16)
        k_ones_rb = sb("k_ones_rb", (1, TF), BF16)
        k_ones_cb = sb("k_ones_cb", (128, 1), BF16)
        k_eye = sb("k_eye", (128, 128))
        k_sel = sb("k_sel", (33, 33 * 128), BF16)
        k_cst = sb("k_cst", (128, 4))
        nc.const_aps.register(F32, EPS, k_cst[:, 0:1])
        nc.const_aps.register(F32, 1.0, k_cst[:, 1:2])
        nc.const_aps.register(F32, -0.5, k_cst[:, 2:3])
        nc.const_aps.register(F32, 0.0, k_cst[:, 3:4])


        block = st.enter_context(nc.Block())
        tk = {}

        kscope = contextlib.ExitStack()
        k_st1 = kscope.enter_context(nc.sbuf_tensor("k_st1", [128, NG, NS], F32))
        k_st2 = kscope.enter_context(nc.sbuf_tensor("k_st2", [128, NG, 33], F32))
        k_st3 = kscope.enter_context(nc.sbuf_tensor("k_st3", [128, ND, NE], F32))
        k_st4 = kscope.enter_context(nc.sbuf_tensor("k_st4", [NE, D], F32))
        k_st5 = kscope.enter_context(nc.sbuf_tensor("k_st5", [1, NE], F32))
        k_st6 = kscope.enter_context(nc.sbuf_tensor("k_st6", [1, D], F32))

        def ph_c_sync(sy):
            for dst, src in [
                (k_conv_w[:], w_conv), (k_conv_b[:], conv_b), (k_dtw[:], dt_w),
                (k_dtb[:], dt_b), (k_st1[:], a_log), (k_d[:], d_par),
                (k_sb1[:], s_b1), (k_eb1[:], e_b1),
                (k_ones_r[:], ones_r[0:1, 0:128]), (k_ones_rb[:], ones_rb),
                (k_ones_cb[:], ones_cb), (k_eye[:], eye),
                (k_st2[:], w_xp), (k_st3[:], r_w),
                (k_st4[:], e_b2), (k_st5[:], r_b), (k_st6[:], s_b2),
                (k_cst[:], cst), (k_sel[:], sel),
            ]:
                sy.dma_start(dst, src).then_inc(d_sy.h, 16)
                d_sy.n += 16
            tk["consts"] = d_sy.n
        block.sync(ph_c_sync)

        def ph_c_scalar(s):
            s.wait_ge(d_sy.h, tk["consts"])
            c_s.inc(s.activation(k_a[:], k_st1[:], ACT.Exp))
            c_s.inc(s.copy(k_xp_b[:], k_st2[:]))
            c_s.inc(s.copy(k_ones_h[:], k_ones_r[:]))
            c_s.inc(s.copy(k_rw_b[:], k_st3[:]))
            c_s.inc(s.copy(k_eb2b[:], k_st4[:]))
            c_s.inc(s.copy(k_rb_b[:], k_st5[:]))
            tk["sc_consts"] = c_s.inc(s.copy(k_sb2b[:], k_st6[:]))
        block.scalar(ph_c_scalar)

        def ph_c_vec(v):
            v.wait_ge(c_s.h, tk["sc_consts"])
            tk["a_neg"] = c_v.inc(v.tensor_scalar(k_a[:], k_a[:], -1.0, None, AL.mult))
        block.vector(ph_c_vec)
        kscope.close()

        cscope = contextlib.ExitStack()
        with cscope:
            cast_stage = cscope.enter_context(nc.sbuf_tensor("cast_stage", [128, 4096], F32))
            cast_bf = cscope.enter_context(nc.sbuf_tensor("cast_bf", [128, 4096], BF16))

            for i in range(24):
                tk[("cast", i)] = c_s.n + i + 1
            ndma_per = [8] * 8 + [2] * 16
            acc = d_sy.n
            for i in range(24):
                acc += 16 * ndma_per[i]
                tk[("cscr", i)] = acc

            def ph_cast_gp(g):
                g.wait_ge(c_s.h, tk["sc_consts"])
                g.wait_ge(c_v.h, tk["a_neg"])
                for i in range(24):
                    if i >= 1:
                        g.wait_ge(c_s.h, tk[("cast", i - 1)])
                    if i < 8:
                        g.dma_start(cast_stage[:],
                                    w_in[i * 128:(i + 1) * 128, :]).then_inc(d_gp.h, 16)
                    else:
                        g.dma_start(cast_stage[:, 0:1024],
                                    w_out[(i - 8) * 128:(i - 7) * 128, :]).then_inc(d_gp.h, 16)
                    d_gp.n += 16
                    tk[("cdma", i)] = d_gp.n
            block.gpsimd(ph_cast_gp)

            def ph_cast_s(s):
                for i in range(24):
                    s.wait_ge(d_gp.h, tk[("cdma", i)])
                    if i >= 1:
                        s.wait_ge(d_sy.h, tk[("cscr", i - 1)])
                    if i < 8:
                        inst = s.copy(cast_bf[:], cast_stage[:])
                    else:
                        inst = s.copy(cast_bf[:, 0:1024], cast_stage[:, 0:1024])
                    assert tk[("cast", i)] == c_s.inc(inst)
            block.scalar(ph_cast_s)

            def ph_cast_sy(sy):
                for i in range(24):
                    sy.wait_ge(c_s.h, tk[("cast", i)])
                    ngrp = 8 if i < 8 else 2
                    for mg in range(ngrp):
                        dst = scr_in[mg, i] if i < 8 else scr_out[mg, i - 8]
                        sy.dma_start(dst, cast_bf[:, mg * 512:(mg + 1) * 512]).then_inc(d_sy.h, 16)
                        d_sy.n += 16
                    assert tk[("cscr", i)] == d_sy.n
                tk["scr_done"] = d_sy.n
            block.sync(ph_cast_sy)
            tk["cast_gp_done"] = d_gp.n

        ms = contextlib.ExitStack()
        with ms:
            def msb(name, shape, dt=F32):
                return ms.enter_context(nc.sbuf_tensor(name, list(shape), dt))

            xc = msb("xc", (128, 2, ND, TC), BF16)
            m_xln = msb("m_xln", (128, ND, TC), BF16)
            w_kt = msb("w_kt", (128, 3, 512), BF16)
            u_pre = msb("u_pre", (128, 4096), BF16)
            x_carry = msb("x_carry", (128, NG, 4), BF16)
            conv_acc = msb("conv_acc", (128, 2, TC), F16)
            x_post = msb("x_post", (128, NG, SC), BF16)
            z_s = msb("z_s", (128, NG, SC), BF16)
            y_g = z_s
            bcdT = msb("bcdT", (33, 2, TC), BF16)
            m_dtg = msb("m_dtg", (128, SC), F16)
            m_wdt = msb("m_wdt", (128, SC), F16)
            brep = msb("brep", (128, NS, SC), F16)
            crep = msb("crep", (128, NS, SC), F16)
            m_da = msb("m_da", (128, 2, SC), F16)
            m_dbx = msb("m_dbx", (128, SC), F16)
            h_buf = msb("h_buf", (128, NS, SC), F16)
            h_last = msb("h_last", (128, NG, NS), F16)
            y_tmp = msb("y_tmp", (128, SC), F16)
            m_dtr = msb("m_dtr", (128, SC), F16)
            f_xln2 = msb("f_xln2", (128, ND, TF), BF16)
            f_sq = msb("f_sq", (128, ND, TF), BF16)
            f_wb = msb("f_wb", (128, 3, 4, 1024), BF16)
            f_he = msb("f_he", (128, NH, TF), BF16)
            f_acc = msb("f_acc", (128, ND, TF), BF16)
            x2s = msb("x2s", (128, ND, SC), BF16)
            f_cwTb = msb("f_cwTb", (NE, TF), BF16)
            f_prob = msb("f_prob", (128, NE))
            f_q = msb("f_q", (128, NE))
            f_m8 = msb("f_m8", (128, 8))
            f_mt = msb("f_mt", (128, 8))
            f_cw = msb("f_cw", (128, NE))
            f_cwrep = msb("f_cwrep", (128, 2, TF), BF16)
            f_out = msb("f_out", (128, 2, TF), BF16)
            f_ra = msb("f_ra", (1, TF))
            f_rb = msb("f_rb", (1, TF))
            f_rc = msb("f_rc", (1, TF))
            f_col = msb("f_col", (128, 1))
            rowa = msb("rowa", (1, TC), F16)
            rowb = msb("rowb", (1, TC), F16)
            rowc = msb("rowc", (1, TC), F16)

            p_proj = ms.enter_context(nc.psum_tensor("p_proj", [128, 2048], F32))
            p_rep = ms.enter_context(nc.psum_tensor("p_rep", [128, 512], F32))
            p_rep2 = ms.enter_context(nc.psum_tensor("p_rep2", [128, 512], F32))
            p_st = ms.enter_context(nc.psum_tensor("p_st", [128, 512], F32))
            p_dtr = ms.enter_context(nc.psum_tensor("p_dtr", [128, 512], F32))

            def xp(g, a, b):
                return u_pre[:, g * TC + a:g * TC + b]

            def mu(dd, a, b):
                return u_pre[:, dd * SC + a:dd * SC + b]

            d_ws = [d_w0, d_w1, d_w2]
            wkt_used = {0: None, 1: None, 2: None}
            rep_free = None
            stat_free = None

            def ln_emit(tag, idx, srcs_bf, srcs_sq, dim, sq_tok):
                nonlocal rep_free

                def f_t(t):
                    t.wait_ge(sq_tok[0].h, sq_tok[1])
                    if rep_free is not None:
                        t.wait_ge(rep_free[0].h, rep_free[1])
                    nk = len(srcs_bf)
                    for k in range(nk):
                        inst = t.matmul(p_st[0:1, 0:TC], k_ones_cb[:], srcs_bf[k],
                                        start=(k == 0), stop=(k == nk - 1))
                    c_t.inc(inst)
                    for k in range(nk):
                        inst = t.matmul(p_st[0:1, TC:2 * TC], k_ones_cb[:], srcs_sq[k],
                                        start=(k == 0), stop=(k == nk - 1))
                    tk[("stat", tag, idx)] = c_t.inc(inst)
                block.tensor(f_t)

                def f_v(v):
                    v.wait_ge(c_t.h, tk[("stat", tag, idx)])
                    t1 = c_v.inc(v.tensor_scalar(rowa[0:1, 0:TC], p_st[0:1, 0:TC],
                                                 1.0 / dim, None, AL.mult))
                    c_v.inc(v.tensor_scalar(rowb[0:1, 0:TC], p_st[0:1, TC:2 * TC],
                                            1.0 / dim, None, AL.mult))
                    v.wait_ge(c_v.h, t1)
                    t3 = c_v.inc(v.tensor_mul(rowc[0:1, 0:TC], rowa[0:1, 0:TC], rowa[0:1, 0:TC]))
                    v.wait_ge(c_v.h, t3)
                    tk[("var", tag, idx)] = c_v.inc(
                        v.tensor_sub(rowb[0:1, 0:TC], rowb[0:1, 0:TC], rowc[0:1, 0:TC]))
                block.vector(f_v)

                def f_s(s):
                    s.wait_ge(c_v.h, tk[("var", tag, idx)])
                    t1 = c_s.inc(s.activation(rowb[0:1, 0:TC], rowb[0:1, 0:TC], ACT.Ln, bias=EPS))
                    s.wait_ge(c_s.h, t1)
                    tk[("rstd", tag, idx)] = c_s.inc(
                        s.activation(rowb[0:1, 0:TC], rowb[0:1, 0:TC], ACT.Exp, scale=-0.5))
                block.scalar(f_s)

                def f_v2(v):
                    v.wait_ge(c_s.h, tk[("rstd", tag, idx)])
                    t1 = c_v.inc(v.tensor_mul(rowc[0:1, 0:TC], rowa[0:1, 0:TC], rowb[0:1, 0:TC]))
                    v.wait_ge(c_v.h, t1)
                    tk[("mrs", tag, idx)] = c_v.inc(
                        v.tensor_scalar(rowc[0:1, 0:TC], rowc[0:1, 0:TC], -1.0, None, AL.mult))
                block.vector(f_v2)

                def f_t2(t):
                    t.wait_ge(c_v.h, tk[("mrs", tag, idx)])
                    c_t.inc(t.matmul(p_rep[:, 0:TC], k_ones_h[0:1, 0:128], rowb[0:1, 0:TC],
                                     start=True, stop=True))
                    tk[("reps", tag, idx)] = c_t.inc(
                        t.matmul(p_rep2[:, 0:TC], k_ones_h[0:1, 0:128], rowc[0:1, 0:TC],
                                 start=True, stop=True))
                block.tensor(f_t2)


            d_fs = [d_f0, d_f1, d_f2]
            fwb_use = {0: None, 1: None, 2: None}
            fwb_tok = {}

            def ffn_rows2(tag, t):
                def fv1(v, tag=tag, t=t):
                    v.wait_ge(c_t.h, tk[("fstat", tag, t)])
                    t1 = c_v.inc(v.tensor_scalar(f_ra[0:1, :], p_st[0:1, 0:TF],
                                                 1.0 / D, None, AL.mult))
                    c_v.inc(v.tensor_scalar(f_rb[0:1, :], p_dtr[0:1, 0:TF],
                                            1.0 / D, None, AL.mult))
                    v.wait_ge(c_v.h, t1)
                    t3 = c_v.inc(v.tensor_mul(f_rc[0:1, :], f_ra[0:1, :], f_ra[0:1, :]))
                    v.wait_ge(c_v.h, t3)
                    tk[("fvar", tag, t)] = c_v.inc(
                        v.tensor_sub(f_rb[0:1, :], f_rb[0:1, :], f_rc[0:1, :]))
                block.vector(fv1)

                def fs1(s, tag=tag, t=t):
                    s.wait_ge(c_v.h, tk[("fvar", tag, t)])
                    t1 = c_s.inc(s.activation(f_rb[0:1, :], f_rb[0:1, :], ACT.Ln, bias=EPS))
                    s.wait_ge(c_s.h, t1)
                    tk[("frstd", tag, t)] = c_s.inc(
                        s.activation(f_rb[0:1, :], f_rb[0:1, :], ACT.Exp, scale=-0.5))
                block.scalar(fs1)

                def fv2(v, tag=tag, t=t):
                    v.wait_ge(c_s.h, tk[("frstd", tag, t)])
                    t1 = c_v.inc(v.tensor_mul(f_rc[0:1, :], f_ra[0:1, :], f_rb[0:1, :]))
                    v.wait_ge(c_v.h, t1)
                    tk[("fmrs", tag, t)] = c_v.inc(
                        v.tensor_scalar(f_rc[0:1, :], f_rc[0:1, :], -1.0, None, AL.mult))
                block.vector(fv2)

                def ft2(t_, tag=tag, t=t):
                    t_.wait_ge(c_v.h, tk[("fmrs", tag, t)])
                    if tag == "ln3":
                        if t + 1 <= 3:
                            t_.wait_ge(c_v.h, tk[("xln", 2 * (t + 1) + 1)])
                        else:
                            t_.wait_ge(c_v.h, tk[("x2done", 3)])
                    else:
                        t_.wait_ge(c_v.h, tk[("xln2", t)])
                    c_t.inc(t_.matmul(p_rep[:, 0:TF], k_ones_r[0:1, 0:128], f_rb[0:1, :],
                                      start=True, stop=True))
                    tk[("freps", tag, t)] = c_t.inc(
                        t_.matmul(p_rep2[:, 0:TF], k_ones_r[0:1, 0:128], f_rc[0:1, :],
                                  start=True, stop=True))
                block.tensor(ft2)

            def emit_ffn_pre(t, s):
                def f3_sq(sc_, t=t, s=s):
                    if t >= 1:
                        sc_.wait_ge(c_t.h, tk[("fstat", "ln4", t - 1)])
                        sc_.wait_ge(c_t.h, tk[("stat", "ln2", 2 * t + 1)])
                    if s <= 3:
                        sc_.wait_ge(c_t.h, tk[("stat", "ln1", 2 * s + 1)])
                    sc_.wait_ge(c_v.h, tk[("x2done", t)])
                    for k in range(ND):
                        c_s.inc(sc_.activation(f_sq[:, k, :], x2s[:, k, :], ACT.Square))
                    tk[("sq3", t)] = c_s.n
                block.scalar(f3_sq)

                def f3_stats(t_, t=t, s=s):
                    t_.wait_ge(c_s.h, tk[("sq3", t)])
                    if s <= 3:
                        t_.wait_ge(c_v.h, tk[("var", "ln1", 2 * s + 1)])
                        t_.wait_ge(c_s.h, tk[("dtrev", s)])
                    else:
                        t_.wait_ge(c_v.h, tk[("var", "ln2", 2 * t + 1)])
                        t_.wait_ge(c_v.h, tk[("fvar", "ln4", t - 1)])
                    for k in range(ND):
                        inst = t_.matmul(p_st[0:1, 0:TF], k_ones_cb[:], x2s[:, k, :],
                                         start=(k == 0), stop=(k == ND - 1))
                    c_t.inc(inst)
                    for k in range(ND):
                        inst = t_.matmul(p_dtr[0:1, 0:TF], k_ones_cb[:], f_sq[:, k, :],
                                         start=(k == 0), stop=(k == ND - 1))
                    tk[("fstat", "ln3", t)] = c_t.inc(inst)
                block.tensor(f3_stats)

                ffn_rows2("ln3", t)

                _vn3 = c_v.n

                def f3_norm(v, t=t, _vn3=_vn3):
                    v.wait_ge(c_v.h, _vn3)
                    v.wait_ge(c_t.h, tk[("freps", "ln3", t)])
                    if t >= 1:
                        v.wait_ge(c_t.h, tk[("w1done", t - 1, 15)])
                    t1 = None
                    for k in range(ND):
                        if t1 is not None:
                            v.wait_ge(c_v.h, t1)
                        t1 = c_v.inc(v.tensor_mul(y_tmp[:], x2s[:, k, :], p_rep[:, 0:TF]))
                        v.wait_ge(c_v.h, t1)
                        t1 = c_v.inc(v.tensor_add(f_xln2[:, k, :], y_tmp[:],
                                                  p_rep2[:, 0:TF]))
                    tk[("xln2", t)] = c_v.n
                block.vector(f3_norm)

                for tt in range(4):
                    def fr_mm(t_, t=t, tt=tt):
                        t_.wait_ge(c_v.h, tk[("xln2", t)])
                        if tt == 0:
                            t_.wait_ge(c_v.h, tk[("fvar", "ln3", t)])
                        prev = tk.get(("cwTevac", t, tt - 1)) if tt > 0 else \
                            tk.get(("cwTevac", t - 1, 3))
                        if prev is not None:
                            t_.wait_ge(c_s.h, prev)
                        for k in range(ND):
                            t_.matmul(p_dtr[0:128, 0:NE],
                                      f_xln2[:, k, tt * 128:(tt + 1) * 128],
                                      k_rw_b[:, k, :], start=(k == 0), stop=False)
                        tk[("logit", t, tt)] = c_t.inc(
                            t_.matmul(p_dtr[0:128, 0:NE], k_ones_rb[0:1, 0:128],
                                      k_rb_b[0:1, :], start=False, stop=True))
                    block.tensor(fr_mm)

                    def fr_sc(s_, t=t, tt=tt):
                        s_.wait_ge(c_t.h, tk[("logit", t, tt)])
                        tk[("loge", t, tt)] = c_s.inc(s_.copy(f_prob[:], p_dtr[0:128, 0:NE]))
                    block.scalar(fr_sc)

                    def fr_max(v, t=t, tt=tt):
                        v.wait_ge(c_s.h, tk[("loge", t, tt)])
                        tk[("rmax", t, tt)] = c_v.inc(
                            v.reduce_max(f_col[:, 0:1], f_prob[:], AX.X, negate=True))
                    block.vector(fr_max)

                    def fr_exp(s_, t=t, tt=tt):
                        s_.wait_ge(c_v.h, tk[("rmax", t, tt)])
                        tk[("rexp", t, tt)] = c_s.inc(
                            s_.activation(f_prob[:], f_prob[:], ACT.Exp, bias=f_col[:, 0:1]))
                    block.scalar(fr_exp)

                    def fr_cw(v, t=t, tt=tt):
                        v.wait_ge(c_s.h, tk[("rexp", t, tt)])
                        if "tapcw" in tk and tk.get(("tapcw_w",)) is None:
                            tk[("tapcw_w",)] = 1
                            v.wait_ge(d_gp.h, tk["tapcw"])
                        t1 = c_v.inc(v.reduce_sum(f_col[:, 0:1], f_prob[:], AX.X))
                        v.wait_ge(c_v.h, t1)
                        t1 = c_v.inc(v.reciprocal(f_col[:, 0:1], f_col[:, 0:1]))
                        v.wait_ge(c_v.h, t1)
                        t1 = c_v.inc(v.tensor_scalar(f_prob[:], f_prob[:], f_col[:, 0:1],
                                                     None, AL.mult))
                        v.wait_ge(c_v.h, t1)
                        t1 = c_v.inc(v.max(f_m8[:], f_prob[:]))
                        c_v.inc(v.memset(f_mt[:], -5.0))
                        v.wait_ge(c_v.h, c_v.n)
                        t1 = c_v.inc(v.tensor_copy(f_mt[:, 0:3], f_m8[:, 0:3]))
                        v.wait_ge(c_v.h, t1)
                        t1 = c_v.inc(v.match_replace(f_q[:], f_mt[:], f_prob[:], 0.0))
                        v.wait_ge(c_v.h, t1)
                        tk[("cw", t, tt)] = c_v.inc(
                            v.tensor_sub(f_cw[:], f_prob[:], f_q[:]))
                    block.vector(fr_cw)

                    def fr_tp(t_, t=t, tt=tt):
                        t_.wait_ge(c_v.h, tk[("cw", t, tt)])
                        tk[("cwT", t, tt)] = c_t.inc(
                            t_.transpose(p_dtr[0:NE, 128:256], f_cw[:], k_eye[:]))
                    block.tensor(fr_tp)

                    def fr_tpe(s_, t=t, tt=tt):
                        s_.wait_ge(c_t.h, tk[("cwT", t, tt)])
                        if t >= 1 and tt == 0:
                            s_.wait_ge(c_t.h, tk[("w2done", t - 1, 15)])
                        tk[("cwTevac", t, tt)] = c_s.inc(
                            s_.copy(f_cwTb[:, tt * 128:(tt + 1) * 128],
                                    p_dtr[0:NE, 128:256]))
                    block.scalar(fr_tpe)

                if "cw0" in taps and t == 0:
                    def f_tap_cw(g):
                        g.wait_ge(c_v.h, tk[("cw", 0, 3)])
                        g.wait_ge(d_gp.h, d_gp.n)
                        g.dma_start(taps["cw0"], f_cw[:]).then_inc(d_gp.h, 16)
                        d_gp.n += 16
                        tk["tapcw"] = d_gp.n
                    block.gpsimd(f_tap_cw)

            def relu_layout(t, snap, interleaved):
                dt18 = 18 if interleaved else 0

                def relub(e):
                    return 8 + (1 if e >= 1 else 0)
                for e in range(16):
                    start = snap + dt18 * (e + 1) + sum(relub(m) for m in range(e))
                    for hmg in range(4):
                        for j in range(2):
                            tk[("reluh", t, e, hmg, j)] = start + 2 * hmg + j + 1
                    if e >= 1:
                        tk[("cwrepe", t, e)] = start + 9
                    tk[("reludone", t, e)] = start + relub(e)

            def acc_layout(t, base, interleaved):
                pos = base
                for k in range(NG):
                    pos += 45 if interleaved else 0
                    if k >= 2:
                        tk[("scale", t, k - 1)] = pos + 1
                        pos += 1
                        for dmg in range(4):
                            for j in range(2):
                                tk[("accvh", t, k - 2, dmg, j)] = pos + 2 * dmg + j + 1
                        pos += 8

            def emit_fwa(t, e):
                def fw(sy, t=t, e=e):
                    sy.wait_ge(d_ce.h, EXPCAST)
                    for (wi, hh) in ((0, 0), (0, 1), (1, 0)):
                        hs = (4 * e + 2 * wi + hh) % 3
                        df = d_fs[hs]
                        if fwb_use[hs] is not None:
                            sy.wait_ge(c_t.h, fwb_use[hs])
                        for kk in range(4):
                            for ho in range(2):
                                sy.dma_start(f_wb[:, hs, kk, ho * 512:(ho + 1) * 512],
                                             scr_e[e, wi, hh * 4 + kk][:, ho * 512:(ho + 1) * 512]
                                             ).then_inc(df.h, 16)
                                df.n += 16
                        fwb_tok[(t, e, wi, hh)] = df.n
                block.sync(fw)

            def emit_fwb(t, e):
                def fw(sy, t=t, e=e):
                    hs = (4 * e + 3) % 3
                    df = d_fs[hs]
                    if fwb_use[hs] is not None:
                        sy.wait_ge(c_t.h, fwb_use[hs])
                    for kk in range(4):
                        for ho in range(2):
                            sy.dma_start(f_wb[:, hs, kk, ho * 512:(ho + 1) * 512],
                                         scr_e[e, 1, 4 + kk][:, ho * 512:(ho + 1) * 512]
                                         ).then_inc(df.h, 16)
                            df.n += 16
                    fwb_tok[(t, e, 1, 1)] = df.n
                block.sync(fw)

            def emit_cwrep(t, e):
                if e == 0:
                    return
                cwrow = e - 1

                def f(t_, t=t, e=e, cwrow=cwrow):
                    if e == 1:
                        t_.wait_ge(c_v.h, tk[("fvar", "ln3", t)])
                    else:
                        t_.wait_ge(c_s.h, tk[("cwrepe2", t, e - 1)])
                    t_.wait_ge(c_s.h, tk[("cwTevac", t, 3)])
                    tk[("cwrepps", t, e)] = c_t.inc(
                        t_.matmul(p_st[:, 0:TF],
                                  k_sel[0:NE, cwrow * 128:(cwrow + 1) * 128],
                                  f_cwTb[:], start=True, stop=True))
                block.tensor(f)

            def emit_w1(t, e, s):
                def f(t_, t=t, e=e, s=s):
                    hsA = (4 * e) % 3
                    hsB = (4 * e + 1) % 3
                    t_.wait_ge(d_fs[hsA].h, fwb_tok[(t, e, 0, 0)])
                    t_.wait_ge(d_fs[hsB].h, fwb_tok[(t, e, 0, 1)])
                    t_.wait_ge(c_v.h, tk[("xln2", t)])
                    if e == 0:
                        if s <= 3:
                            t_.wait_ge(c_s.h, tk[("evac", 2 * s + 1, 7)])
                        else:
                            t_.wait_ge(c_s.h, tk[("uevac", 3, 1)])
                    for hmg in range(4):
                        for j in range(2):
                            if hmg >= 1:
                                t_.wait_ge(c_s.h, tk[("reluh", t, e, hmg - 1, j)])
                            elif e >= 1:
                                t_.wait_ge(c_s.h, tk[("reluh", t, e - 1, 3, j)])
                            for k in range(8):
                                hs = hsA if k < 4 else hsB
                                inst = t_.matmul(
                                    p_proj[:, j * TF:(j + 1) * TF],
                                    f_wb[:, hs, k % 4,
                                         (hmg * 2 + j) * 128:(hmg * 2 + j + 1) * 128],
                                    f_xln2[:, k, :], start=(k == 0), stop=(k == 7))
                            c_t.inc(inst)
                            tk[("w1ps", t, e, hmg, j)] = c_t.n
                    tk[("w1done", t, e)] = c_t.n
                    fwb_use[hsA] = c_t.n
                    fwb_use[hsB] = c_t.n
                block.tensor(f)

            def emit_relu(t, e):
                def f(s_, t=t, e=e):
                    b1 = e_srcs[e][2]
                    for hmg in range(4):
                        for j in range(2):
                            m = hmg * 2 + j
                            s_.wait_ge(c_t.h, tk[("w1ps", t, e, hmg, j)])
                            if hmg == 0 and e >= 1:
                                s_.wait_ge(c_t.h, tk[("w2done", t, e - 1)])
                            assert tk[("reluh", t, e, hmg, j)] == c_s.inc(
                                s_.activation(f_he[:, m, :],
                                              p_proj[:, j * TF:(j + 1) * TF],
                                              ACT.Relu, bias=b1[:, m:m + 1]))
                    if e >= 1:
                        s_.wait_ge(c_t.h, tk[("cwrepps", t, e)])
                        if e >= 3:
                            s_.wait_ge(c_v.h, tk[("accdone", t, e - 2)])
                        assert tk[("cwrepe", t, e)] == c_s.inc(
                            s_.copy(f_cwrep[:, e % 2, :], p_st[:, 0:TF]))
                        tk[("cwrepe2", t, e)] = tk[("cwrepe", t, e)]
                    assert tk[("reludone", t, e)] == c_s.n
                block.scalar(f)


            def emit_w2(t, e, s):
                def f(t_, t=t, e=e, s=s):
                    hsC = (4 * e + 2) % 3
                    hsD = (4 * e + 3) % 3
                    t_.wait_ge(d_fs[hsC].h, fwb_tok[(t, e, 1, 0)])
                    t_.wait_ge(d_fs[hsD].h, fwb_tok[(t, e, 1, 1)])
                    t_.wait_ge(c_s.h, tk[("reludone", t, e)])
                    if e == 0:
                        if s <= 3:
                            t_.wait_ge(c_s.h, tk[("bcd", 2 * s + 1)])
                        else:
                            t_.wait_ge(c_s.h, tk[("uevac", 3, 1)])
                    is_shared = (e == 0)
                    is_last = (e == 15)
                    for dmg in range(4):
                        for j in range(2):
                            if e >= 1 or dmg >= 1:
                                prev = tk.get(("accvh", t, e, dmg - 1, j)) if dmg >= 1 \
                                    else tk.get(("accvh", t, e - 1, 3, j))
                                if prev is not None:
                                    t_.wait_ge(c_v.h, prev)
                            for k in range(NH):
                                stop = (k == NH - 1) and not (is_shared or is_last)
                                inst = t_.matmul(
                                    p_proj[:, 1024 + j * TF:1024 + (j + 1) * TF],
                                    f_wb[:, hsC if k < 4 else hsD, k % 4,
                                         (dmg * 2 + j) * 128:(dmg * 2 + j + 1) * 128],
                                    f_he[:, k, :], start=(k == 0), stop=stop)
                            m = dmg * 2 + j
                            if is_shared:
                                inst = t_.matmul(p_proj[:, 1024 + j * TF:1024 + (j + 1) * TF],
                                                 k_sb2b[0:1, m * 128:(m + 1) * 128],
                                                 k_ones_rb[0:1, 0:TF], start=False, stop=True)
                            elif is_last:
                                inst = t_.matmul(p_proj[:, 1024 + j * TF:1024 + (j + 1) * TF],
                                                 k_eb2b[:, m * 128:(m + 1) * 128],
                                                 f_cwTb[:], start=False, stop=True)
                            c_t.inc(inst)
                            tk[("w2ps", t, e, dmg, j)] = c_t.n
                    tk[("w2done", t, e)] = c_t.n
                    fwb_use[hsC] = c_t.n
                    fwb_use[hsD] = c_t.n
                block.tensor(f)

            def emit_acc(t, e):
                _vn = c_v.n

                def f(v, t=t, e=e, _vn=_vn):
                    v.wait_ge(c_v.h, _vn)
                    if e == 0 and t >= 1:
                        v.wait_ge(c_v.h, tk[("fout", t - 1, ND - 1)])
                    for dmg in range(4):
                        for j in range(2):
                            m = dmg * 2 + j
                            v.wait_ge(c_t.h, tk[("w2ps", t, e, dmg, j)])
                            if e == 0:
                                tok = c_v.inc(v.tensor_copy(
                                    f_acc[:, m, :], p_proj[:, 1024 + j * TF:1024 + (j + 1) * TF]))
                                assert tk[("accvh", t, e, dmg, j)] == tok
                            else:
                                bc = f_cwrep[:, e % 2, :]
                                tok = c_v.inc(v.tensor_mul(
                                    y_tmp[:], p_proj[:, 1024 + j * TF:1024 + (j + 1) * TF], bc))
                                assert tk[("accvh", t, e, dmg, j)] == tok
                                v.wait_ge(c_v.h, tok)
                                c_v.inc(v.tensor_add(f_acc[:, m, :], y_tmp[:], f_acc[:, m, :]))
                                v.wait_ge(c_v.h, c_v.n)
                    tk[("accdone", t, e)] = c_v.n
                block.vector(f)

            def emit_ffn_post(t, s):
                def f4_sq(s_, t=t):
                    s_.wait_ge(c_v.h, tk[("accdone", t, 15)])
                    for k in range(ND):
                        c_s.inc(s_.activation(f_sq[:, k, :], f_acc[:, k, :], ACT.Square))
                    tk[("sq4", t)] = c_s.n
                block.scalar(f4_sq)

                def f4_stats(t_, t=t):
                    t_.wait_ge(c_s.h, tk[("sq4", t)])
                    t_.wait_ge(c_s.h, tk[("cwrepe2", t, 15)])
                    t_.wait_ge(c_s.h, tk[("loge", t, 3)])
                    for k in range(ND):
                        inst = t_.matmul(p_st[0:1, 0:TF], k_ones_cb[:], f_acc[:, k, :],
                                         start=(k == 0), stop=(k == ND - 1))
                    c_t.inc(inst)
                    for k in range(ND):
                        inst = t_.matmul(p_dtr[0:1, 0:TF], k_ones_cb[:], f_sq[:, k, :],
                                         start=(k == 0), stop=(k == ND - 1))
                    tk[("fstat", "ln4", t)] = c_t.inc(inst)
                block.tensor(f4_stats)

                ffn_rows2("ln4", t)

                for k in range(ND):
                    co = c_o0 if k % 2 == 0 else c_o1
                    tk[("odma", t, k)] = co.n + 16 * (k // 2 + 1)

                _vn4 = c_v.n

                def f4_fin(v, t=t, _vn4=_vn4):
                    v.wait_ge(c_v.h, _vn4)
                    v.wait_ge(c_t.h, tk[("freps", "ln4", t)])
                    t1 = None
                    for k in range(ND):
                        prev = tk.get(("odma", t, k - 2)) if k >= 2 else \
                            tk.get(("odma", t - 1, ND - 2 + k))
                        if prev is not None:
                            v.wait_ge((c_o0 if k % 2 == 0 else c_o1).h, prev)
                        if t1 is not None:
                            v.wait_ge(c_v.h, t1)
                        t1 = c_v.inc(v.tensor_mul(y_tmp[:], f_acc[:, k, :], p_rep[:, 0:TF]))
                        v.wait_ge(c_v.h, t1)
                        t1 = c_v.inc(v.tensor_add(y_tmp[:], y_tmp[:], p_rep2[:, 0:TF]))
                        v.wait_ge(c_v.h, t1)
                        t1 = c_v.inc(v.tensor_add(f_out[:, k % 2, :], x2s[:, k, :], y_tmp[:]))
                        tk[("fout", t, k)] = t1
                block.vector(f4_fin)

                def f4_dma(g, t=t):
                    for k in range(ND):
                        co = c_o0 if k % 2 == 0 else c_o1
                        g.wait_ge(c_v.h, tk[("fout", t, k)])
                        g.dma_start(out_t[k * 128:(k + 1) * 128, t * TF:(t + 1) * TF],
                                    f_out[:, k % 2, :]).then_inc(co.h, 16)
                        co.n += 16
                        assert tk[("odma", t, k)] == co.n
                block.gpsimd(f4_dma)

            e_srcs = [(s_w1, s_w2, k_sb1, None)] + \
                     [(e_w1[e], e_w2[e], k_eb1[:, e, :], e) for e in range(NE)]
            EXPCAST = 16 * 2 * 8 * 16

            for c in range(NCH):
                sc_i = c // 2
                par = c % 2
                t0 = c * TC

                if c == 2:
                    def f_ecast(g):
                        for ei in range(16):
                            w1d, w2d = e_srcs[ei][0], e_srcs[ei][1]
                            for wi, wd in ((0, w1d), (1, w2d)):
                                for kt in range(8):
                                    g.dma_start(scr_e[ei, wi, kt],
                                                wd[kt * 128:(kt + 1) * 128, :]
                                                ).then_inc(d_ce.h, 16)
                                    d_ce.n += 16
                        assert d_ce.n == EXPCAST
                    block.gpsimd(f_ecast)

                def f_xdma(g, c=c, par=par, t0=t0):
                    g.wait_ge(d_gp.h, d_gp.n)
                    if c < 2:
                        g.wait_ge(d_sy.h, tk["scr_done"])
                    else:
                        g.wait_ge(c_v.h, tk[("x2done", (c - 2) // 2)])
                    for dtile in range(ND):
                        g.dma_start(xc[:, par, dtile, :],
                                    x_in[dtile * 128:(dtile + 1) * 128, t0:t0 + TC]
                                    ).then_inc(d_gp.h, 16)
                        d_gp.n += 16
                    tk[("xdma", c)] = d_gp.n
                block.gpsimd(f_xdma)

                def f_ln1sq(s, c=c, par=par):
                    s.wait_ge(d_gp.h, tk[("xdma", c)])
                    if stat_free is not None:
                        s.wait_ge(stat_free[0].h, stat_free[1])
                    if par == 0 and sc_i >= 2:
                        s.wait_ge(c_t.h, tk[("fstat", "ln4", sc_i - 2)])
                    for dtile in range(ND):
                        c_s.inc(s.activation(f_sq[:, dtile, 0:TC], xc[:, par, dtile, :],
                                             ACT.Square))
                    tk[("sq", "ln1", c)] = c_s.n
                block.scalar(f_ln1sq)

                ln_emit("ln1", c, [xc[:, par, k, :] for k in range(ND)],
                        [f_sq[:, k, 0:TC] for k in range(ND)], D,
                        (c_s, tk[("sq", "ln1", c)]))
                stat_free = (c_t, tk[("stat", "ln1", c)])

                def f_ln1n(v, c=c, par=par):
                    v.wait_ge(c_t.h, tk[("reps", "ln1", c)])
                    if c >= 1:
                        v.wait_ge(c_t.h, tk[("inproj", c - 1)])
                    for dtile in range(ND):
                        t1 = c_v.inc(v.tensor_mul(m_xln[:, dtile, :], xc[:, par, dtile, :],
                                                  p_rep[:, 0:TC]))
                        v.wait_ge(c_v.h, t1)
                        c_v.inc(v.tensor_add(m_xln[:, dtile, :], m_xln[:, dtile, :],
                                             p_rep2[:, 0:TC]))
                    tk[("xln", c)] = c_v.n
                block.vector(f_ln1n)
                rep_free = (c_v, tk[("xln", c)])

                for mg in range(8):
                    _bt = c_t.n
                    for k in range(8):
                        tk[("wku", c, mg, k)] = _bt + (k + 1 if k < 7 else 11)

                    def f_wdma(sy, c=c, mg=mg):
                        sy.wait_ge(d_sy.h, tk["scr_done"])
                        for k in range(8):
                            wpar = k % 3
                            dw = d_ws[wpar]
                            if k < 3:
                                if wkt_used[wpar] is not None:
                                    sy.wait_ge(c_t.h, wkt_used[wpar])
                            else:
                                sy.wait_ge(c_t.h, tk[("wku", c, mg, k - 3)])
                            sy.dma_start(w_kt[:, wpar, :], scr_in[mg, k]).then_inc(dw.h, 16)
                            dw.n += 16
                            tk[("wk", c, mg, k)] = dw.n
                    block.sync(f_wdma)

                    def f_mm(t, c=c, mg=mg, par=par, sc_i=sc_i):
                        t.wait_ge(c_v.h, tk[("xln", c)])
                        if mg >= 1:
                            t.wait_ge(c_s.h, tk[("evac", c, mg - 1)])
                        elif c >= 1:
                            t.wait_ge(c_s.h, tk[("evac", c - 1, 7)])
                            t.wait_ge(c_s.h, tk[("bcd", c - 1)])
                        if mg == 0 and par == 0 and sc_i >= 1:
                            t.wait_ge(c_s.h, tk[("uevac", sc_i - 1, 1)])
                        for k in range(8):
                            wpar = k % 3
                            dw = d_ws[wpar]
                            t.wait_ge(dw.h, tk[("wk", c, mg, k)])
                            for j in range(4):
                                inst = t.matmul(p_proj[:, j * 512:j * 512 + TC],
                                                w_kt[:, wpar, j * 128:(j + 1) * 128],
                                                m_xln[:, k, :], start=(k == 0), stop=(k == 7))
                                if k == 7:
                                    c_t.inc(inst)
                            if k < 7:
                                assert tk[("wku", c, mg, k)] == c_t.inc(inst)
                            else:
                                assert tk[("wku", c, mg, k)] == c_t.n
                        for _p in range(3):
                            _lk = max(kk for kk in range(8) if kk % 3 == _p)
                            wkt_used[_p] = tk[("wku", c, mg, _lk)]
                        tk[("ippsum", c, mg)] = c_t.n
                    block.tensor(f_mm)

                    def f_evac(s, c=c, mg=mg, par=par, sc_i=sc_i):
                        s.wait_ge(c_t.h, tk[("ippsum", c, mg)])
                        if mg < 4:
                            if c >= 1:
                                s.wait_ge(c_v.h, tk[("convdone", c - 1)])
                            if c >= 2:
                                s.wait_ge(c_v.h, tk[("x2done", sc_i - 1)])
                        else:
                            if par == 0 and sc_i >= 1:
                                s.wait_ge(c_t.h, tk[("uproj", sc_i - 1)])
                        for j in range(4):
                            m = mg * 4 + j
                            if mg < 4:
                                c_s.inc(s.copy(xp(m, 0, TC), p_proj[:, j * 512:j * 512 + TC]))
                            else:
                                c_s.inc(s.activation(z_s[:, m - 16, par * TC:(par + 1) * TC],
                                                     p_proj[:, j * 512:j * 512 + TC], ACT.Silu))
                        tk[("evac", c, mg)] = c_s.n
                    block.scalar(f_evac)
                tk[("inproj", c)] = c_t.n

                if "xpre0" in taps and c == 0:
                    def f_tap_xpre(g):
                        g.wait_ge(c_s.h, tk[("evac", 0, 3)])
                        g.wait_ge(d_gp.h, d_gp.n)
                        g.dma_start(taps["xpre0"], u_pre[:]).then_inc(d_gp.h, 16)
                        d_gp.n += 16
                    block.gpsimd(f_tap_xpre)

                for g_ in range(NG):
                    tk[("silu", c, g_)] = c_s.n + g_ + 1
                tk[("siludone", c)] = c_s.n + NG

                def f_conv(v, c=c):
                    v.wait_ge(c_s.h, tk[("evac", c, 3)])
                    if c >= 1:
                        v.wait_ge(c_s.h, tk[("siludone", c - 1)])
                    for g_ in range(NG):
                        cp = g_ % 2
                        if g_ >= 2:
                            v.wait_ge(c_s.h, tk[("silu", c, g_ - 2)])
                        t1 = c_v.inc(v.tensor_scalar(conv_acc[:, cp, :], xp(g_, 0, TC),
                                                     k_conv_w[:, g_, 3:4], None, AL.mult))
                        for kk in range(3):
                            off = 3 - kk
                            v.wait_ge(c_v.h, t1)
                            t1 = c_v.inc(v.scalar_tensor_tensor(
                                conv_acc[:, cp, off:TC], xp(g_, 0, TC - off),
                                k_conv_w[:, g_, kk:kk + 1],
                                conv_acc[:, cp, off:TC], AL.mult, AL.add))
                            if c > 0:
                                v.wait_ge(c_v.h, t1)
                                t1 = c_v.inc(v.scalar_tensor_tensor(
                                    conv_acc[:, cp, 0:off], x_carry[:, g_, 4 - off:4],
                                    k_conv_w[:, g_, kk:kk + 1],
                                    conv_acc[:, cp, 0:off], AL.mult, AL.add))
                        tk[("conv", c, g_)] = t1
                    v.wait_ge(c_v.h, tk[("conv", c, NG - 1)])
                    tk[("convdone", c)] = c_v.inc(
                        v.tensor_copy(x_carry[:, :, 1:4], u_pre[:].rearrange('p (g t) -> p g t', g=NG)[:, :, TC - 3:TC]))
                block.vector(f_conv)

                def f_silu(s, c=c, par=par, sc_i=sc_i):
                    if par == 0 and sc_i >= 1:
                        s.wait_ge(c_v.h, tk[("ygdone", sc_i - 1)])
                    for g_ in range(NG):
                        cp = g_ % 2
                        s.wait_ge(c_v.h, tk[("conv", c, g_)])
                        inst = s.activation(x_post[:, g_, par * TC:(par + 1) * TC],
                                            conv_acc[:, cp, :], ACT.Silu,
                                            bias=k_conv_b[:, g_:g_ + 1])
                        assert tk[("silu", c, g_)] == c_s.inc(inst)
                    assert tk[("siludone", c)] == c_s.n
                block.scalar(f_silu)

                def f_xproj(t, c=c, par=par, sc_i=sc_i):
                    t.wait_ge(c_s.h, tk[("siludone", c)])
                    if c >= 1:
                        t.wait_ge(c_s.h, tk[("bcd", c - 1)])
                    if par == 0 and sc_i >= 1:
                        t.wait_ge(c_s.h, tk[("uevac", sc_i - 1, 1)])
                    for k in range(NG):
                        inst = t.matmul(p_proj[0:33, 1792:2048],
                                        k_xp_b[:, k, :], x_post[:, k, par * TC:(par + 1) * TC],
                                        start=(k == 0), stop=(k == NG - 1))
                    tk[("bcdps", c)] = c_t.inc(inst)
                block.tensor(f_xproj)

                def f_bcdevac(s, c=c, par=par, sc_i=sc_i):
                    s.wait_ge(c_t.h, tk[("bcdps", c)])
                    if sc_i >= 1 and par == 0:
                        s.wait_ge(c_t.h, tk[("creps", sc_i - 1)])
                    tk[("bcd", c)] = c_s.inc(
                        s.copy(bcdT[:, par, :], p_proj[0:33, 1792:2048]))
                block.scalar(f_bcdevac)

                if par == 0:
                    continue

                st0 = sc_i * SC

                if "bcd0" in taps and sc_i == 0:
                    def f_tap_bcd(g):
                        g.wait_ge(c_s.h, tk[("bcd", 1)])
                        g.wait_ge(d_gp.h, d_gp.n)
                        g.dma_start(taps["bcd0"], bcdT[:]).then_inc(d_gp.h, 16)
                        d_gp.n += 16
                    block.gpsimd(f_tap_bcd)

                def f_dtr(t, sc_i=sc_i, c=c):
                    t.wait_ge(c_s.h, tk[("bcd", c)])
                    if sc_i >= 1:
                        t.wait_ge(c_s.h, tk[("dtrev", sc_i - 1)])
                        t.wait_ge(c_s.h, tk[("dtdone", sc_i - 1)])
                    if sc_i >= 2:
                        t.wait_ge(c_v.h, tk[("fvar", "ln4", sc_i - 2)])
                    for p2 in range(2):
                        c_t.inc(t.matmul(p_dtr[:, p2 * TC:(p2 + 1) * TC],
                                         k_sel[:, 32 * 128:33 * 128],
                                         bcdT[:, p2, :], start=True, stop=True))
                    tk[("dtr", sc_i)] = c_t.n
                block.tensor(f_dtr)

                def f_dtrevac(s, sc_i=sc_i):
                    s.wait_ge(c_t.h, tk[("dtr", sc_i)])
                    tk[("dtrev", sc_i)] = c_s.inc(s.copy(m_dtr[:], p_dtr[:, 0:SC]))
                block.scalar(f_dtrevac)

                _bsev = c_s.n
                for n in range(NS):
                    tk[("brep", sc_i, n)] = _bsev + n + 1

                def f_breps(t, sc_i=sc_i):
                    nonlocal rep_free
                    if rep_free is not None:
                        t.wait_ge(rep_free[0].h, rep_free[1])
                    if sc_i >= 1:
                        t.wait_ge(c_v.h, tk[("scandone", sc_i - 1)])
                    for n in range(NS):
                        if n >= 1:
                            t.wait_ge(c_s.h, tk[("brep", sc_i, n - 1)])
                        for p2 in range(2):
                            c_t.inc(t.matmul(p_rep[:, p2 * TC:(p2 + 1) * TC],
                                             k_sel[:, n * 128:(n + 1) * 128],
                                             bcdT[:, p2, :], start=True, stop=True))
                        tk[("brp", sc_i, n)] = c_t.n
                block.tensor(f_breps)

                def f_brevac(s, sc_i=sc_i):
                    for n in range(NS):
                        s.wait_ge(c_t.h, tk[("brp", sc_i, n)])
                        assert tk[("brep", sc_i, n)] == c_s.inc(
                            s.copy(brep[:, n, :], p_rep[:, 0:SC]))
                block.scalar(f_brevac)

                _csev = c_s.n
                for n in range(NS):
                    tk[("crep", sc_i, n)] = _csev + n + 1

                def f_creps(t, sc_i=sc_i):
                    t.wait_ge(c_s.h, tk[("brep", sc_i, NS - 1)])
                    if sc_i >= 1:
                        t.wait_ge(c_v.h, tk[("ygdone", sc_i - 1)])
                    for n in range(NS):
                        if n >= 1:
                            t.wait_ge(c_s.h, tk[("crep", sc_i, n - 1)])
                        for p2 in range(2):
                            c_t.inc(t.matmul(p_rep[:, p2 * TC:(p2 + 1) * TC],
                                             k_sel[:, (16 + n) * 128:(17 + n) * 128],
                                             bcdT[:, p2, :], start=True, stop=True))
                        tk[("crp", sc_i, n)] = c_t.n
                    tk[("creps", sc_i)] = c_t.n
                block.tensor(f_creps)

                def f_crevac(s, sc_i=sc_i):
                    for n in range(NS):
                        s.wait_ge(c_t.h, tk[("crp", sc_i, n)])
                        assert tk[("crep", sc_i, n)] == c_s.inc(
                            s.copy(crep[:, n, :], p_rep[:, 0:SC]))
                block.scalar(f_crevac)
                rep_free = (c_s, tk[("crep", sc_i, NS - 1)])

                _t = sc_i - 1 if sc_i >= 1 else None
                if _t is not None:
                    emit_ffn_pre(_t, sc_i)
                    relu_layout(_t, c_s.n, True)

                def acc_tokens(t_, e_, pos):
                    if e_ == 0:
                        for dmg in range(4):
                            for jj in range(2):
                                tk[("accvh", t_, e_, dmg, jj)] = pos + 2 * dmg + jj + 1
                        return pos + 8
                    for dmg in range(4):
                        for jj in range(2):
                            tk[("accvh", t_, e_, dmg, jj)] = pos + 2 * (2 * dmg + jj) + 1
                    return pos + 16

                _pos = c_v.n
                for k_ in range(NG):
                    if _t is not None and k_ >= 1:
                        _pos = acc_tokens(_t, k_ - 1, _pos)
                    tk[("wdt", sc_i, k_)] = _pos + 1
                    for n in range(NS):
                        tk[("dbx", sc_i, k_ * NS + n)] = _pos + 2 + 2 * n
                        tk[("scan", sc_i, k_ * NS + n)] = _pos + 3 + 2 * n
                    tk[("hcp", sc_i, k_)] = _pos + 34
                    tk[("yg", sc_i, k_)] = _pos + 45
                    _pos += 45
                if _t is not None:
                    _pos = acc_tokens(_t, NG - 1, _pos)
                tk[("scandone", sc_i)] = tk[("hcp", sc_i, NG - 1)]
                tk[("ygdone", sc_i)] = tk[("yg", sc_i, NG - 1)]

                def mk_dtda_g(g_, sc_i=sc_i):
                    def f(s, sc_i=sc_i, g_=g_):
                        if g_ == 0:
                            s.wait_ge(c_s.h, tk[("dtrev", sc_i)])
                        if g_ >= 1:
                            s.wait_ge(c_v.h, tk[("wdt", sc_i, g_ - 1)])
                        if g_ >= 1:
                            s.wait_ge(c_s.h, tk[("da", sc_i, g_ * NS - 1)])
                        t1 = c_s.inc(s.activation(m_dtg[:], m_dtr[:], ACT.Exp,
                                                  scale=k_dtw[:, g_:g_ + 1],
                                                  bias=k_dtb[:, g_:g_ + 1]))
                        s.wait_ge(c_s.h, t1)
                        tk[("dt", sc_i, g_)] = c_s.inc(
                            s.activation(m_dtg[:], m_dtg[:], ACT.Ln, bias=1.0))
                        s.wait_ge(c_s.h, tk[("dt", sc_i, g_)])
                        for n in range(NS):
                            i = g_ * NS + n
                            dpar = i % 2
                            prev = tk.get(("scan", sc_i, i - 2)) if i >= 2 else \
                                (tk.get(("scan", sc_i - 1, 254 + i)) if sc_i >= 1 else None)
                            if prev is not None:
                                s.wait_ge(c_v.h, prev)
                            tk[("da", sc_i, i)] = c_s.inc(
                                s.activation(m_da[:, dpar, :], m_dtg[:], ACT.Exp,
                                             scale=k_a[:, g_, n:n + 1]))
                        if g_ == NG - 1:
                            tk[("dtdone", sc_i)] = c_s.n
                    block.scalar(f)

                def mk_scan_g(g_, sc_i=sc_i):
                    def f(v, sc_i=sc_i, g_=g_):
                        if g_ == 0:
                            v.wait_ge(c_s.h, tk[("evac", 2 * sc_i + 1, 7)])
                            v.wait_ge(c_s.h, tk[("siludone", 2 * sc_i + 1)])
                            if sc_i >= 1:
                                v.wait_ge(c_t.h, tk[("uproj", sc_i - 1)])
                            v.wait_ge(c_s.h, tk[("brep", sc_i, NS - 1)])
                        v.wait_ge(c_s.h, tk[("dt", sc_i, g_)])
                        assert tk[("wdt", sc_i, g_)] == c_v.inc(
                            v.tensor_mul(m_wdt[:], m_dtg[:], x_post[:, g_, :]))
                        for n in range(NS):
                            i = g_ * NS + n
                            dpar = i % 2
                            v.wait_ge(c_s.h, tk[("da", sc_i, i)])
                            v.wait_ge(c_v.h, tk[("wdt", sc_i, g_)])
                            if n >= 1:
                                v.wait_ge(c_v.h, tk[("scan", sc_i, i - 1)])
                            if g_ == 0 and n == 0:
                                v.wait_ge(c_s.h, tk[("crep", sc_i, NS - 1)])
                            assert tk[("dbx", sc_i, i)] == c_v.inc(
                                v.tensor_mul(m_dbx[:], m_wdt[:], brep[:, n, :]))
                            v.wait_ge(c_v.h, tk[("dbx", sc_i, i)])
                            init = 0.0 if sc_i == 0 else h_last[:, g_, n:n + 1]
                            assert tk[("scan", sc_i, i)] == c_v.inc(
                                v.tensor_tensor_scan(h_buf[:, n, :], m_da[:, dpar, :],
                                                     m_dbx[:], init, AL.mult, AL.add))
                        v.wait_ge(c_v.h, tk[("scan", sc_i, g_ * NS + NS - 1)])
                        assert tk[("hcp", sc_i, g_)] == c_v.inc(
                            v.tensor_copy(h_last[:, g_, :], h_buf[:, :, SC - 1]))
                        v.wait_ge(c_v.h, tk[("hcp", sc_i, g_)])
                        for q in range(4):
                            t1 = c_v.inc(v.tensor_mul(h_buf[:, 4 * q:4 * q + 4, :],
                                                      h_buf[:, 4 * q:4 * q + 4, :],
                                                      crep[:, 4 * q:4 * q + 4, :]))
                        v.wait_ge(c_v.h, t1)
                        c_v.inc(v.tensor_add(h_buf[:, 0:4, :], h_buf[:, 0:4, :],
                                             h_buf[:, 8:12, :]))
                        t1 = c_v.inc(v.tensor_add(h_buf[:, 4:8, :], h_buf[:, 4:8, :],
                                                  h_buf[:, 12:16, :]))
                        v.wait_ge(c_v.h, t1)
                        t1 = c_v.inc(v.tensor_add(h_buf[:, 0:4, :], h_buf[:, 0:4, :],
                                                  h_buf[:, 4:8, :]))
                        v.wait_ge(c_v.h, t1)
                        t1 = c_v.inc(v.tensor_add(h_buf[:, 0:2, :], h_buf[:, 0:2, :],
                                                  h_buf[:, 2:4, :]))
                        v.wait_ge(c_v.h, t1)
                        t1 = c_v.inc(v.tensor_add(h_buf[:, 0, :], h_buf[:, 0, :],
                                                  h_buf[:, 1, :]))
                        v.wait_ge(c_v.h, t1)
                        t1 = c_v.inc(v.scalar_tensor_tensor(y_tmp[:], x_post[:, g_, :],
                                                            k_d[:, g_:g_ + 1],
                                                            h_buf[:, 0, :], AL.mult, AL.add))
                        v.wait_ge(c_v.h, t1)
                        assert tk[("yg", sc_i, g_)] == c_v.inc(
                            v.tensor_mul(y_g[:, g_, :], y_tmp[:], z_s[:, g_, :]))
                    block.vector(f)

                for k_ in range(NG):
                    if _t is not None and k_ >= 1:
                        emit_relu(_t, k_ - 1)
                    mk_dtda_g(k_)
                    if _t is not None:
                        emit_cwrep(_t, k_)
                        if k_ >= 1:
                            emit_w2(_t, k_ - 1, sc_i)
                        emit_fwa(_t, k_)
                        emit_w1(_t, k_, sc_i)
                        emit_fwb(_t, k_)
                    if _t is not None and k_ >= 1:
                        emit_acc(_t, k_ - 1)
                    mk_scan_g(k_)

                if _t is not None:
                    emit_relu(_t, NG - 1)
                    emit_w2(_t, NG - 1, sc_i)
                    emit_acc(_t, NG - 1)
                    emit_ffn_post(_t, sc_i)
                    stat_free = (c_v, tk[("fvar", "ln4", _t)])
                    rep_free = (c_v, tk[("fout", _t, ND - 1)])

                if "xpost0" in taps and sc_i == 0:
                    def f_tap_xpost(g):
                        g.wait_ge(c_s.h, tk[("siludone", 1)])
                        g.wait_ge(d_gp.h, d_gp.n)
                        g.dma_start(taps["xpost0"], x_post[:]).then_inc(d_gp.h, 16)
                        d_gp.n += 16
                    block.gpsimd(f_tap_xpost)
                if "yg0" in taps and sc_i == 0:
                    def f_tap_yg(g):
                        g.wait_ge(c_v.h, tk[("ygdone", 0)])
                        g.wait_ge(d_gp.h, d_gp.n)
                        g.dma_start(taps["yg0"], y_g[:]).then_inc(d_gp.h, 16)
                        d_gp.n += 16
                    block.gpsimd(f_tap_yg)

                for half in range(2):
                    _bt = c_t.n
                    for k in range(NG):
                        tk[("oku", sc_i, half, k)] = _bt + (k + 1 if k < NG - 1 else NG + 3)

                    def f_odma(sy, sc_i=sc_i, half=half):
                        for k in range(NG):
                            wpar = k % 3
                            dw = d_ws[wpar]
                            if k < 3:
                                if wkt_used[wpar] is not None:
                                    sy.wait_ge(c_t.h, wkt_used[wpar])
                            else:
                                sy.wait_ge(c_t.h, tk[("oku", sc_i, half, k - 3)])
                            sy.dma_start(w_kt[:, wpar, :], scr_out[half, k]).then_inc(dw.h, 16)
                            dw.n += 16
                            tk[("ok", sc_i, half, k)] = dw.n
                    block.sync(f_odma)

                    def f_omm(t, sc_i=sc_i, half=half):
                        if half == 0:
                            t.wait_ge(c_s.h, tk[("bcd", 2 * sc_i + 1)])
                            t.wait_ge(c_s.h, tk[("evac", 2 * sc_i + 1, 7)])
                            if sc_i >= 1:
                                t.wait_ge(c_s.h, tk[("reludone", sc_i - 1, NG - 1)])
                                t.wait_ge(c_v.h, tk[("accdone", sc_i - 1, 15)])
                        else:
                            t.wait_ge(c_s.h, tk[("uevac", sc_i, 0)])
                        for k in range(NG):
                            wpar = k % 3
                            dw = d_ws[wpar]
                            t.wait_ge(dw.h, tk[("ok", sc_i, half, k)])
                            t.wait_ge(c_v.h, tk[("yg", sc_i, k)])
                            for j in range(4):
                                inst = t.matmul(p_proj[:, j * SC:(j + 1) * SC],
                                                w_kt[:, wpar, j * 128:(j + 1) * 128],
                                                y_g[:, k, :], start=(k == 0), stop=(k == NG - 1))
                                if k == NG - 1:
                                    c_t.inc(inst)
                            if k < NG - 1:
                                assert tk[("oku", sc_i, half, k)] == c_t.inc(inst)
                            else:
                                assert tk[("oku", sc_i, half, k)] == c_t.n
                        for _p in range(3):
                            _lk = max(kk for kk in range(NG) if kk % 3 == _p)
                            wkt_used[_p] = tk[("oku", sc_i, half, _lk)]
                        tk[("ups", sc_i, half)] = c_t.n
                    block.tensor(f_omm)

                    def f_uevac(s, sc_i=sc_i, half=half):
                        s.wait_ge(c_t.h, tk[("ups", sc_i, half)])
                        s.wait_ge(c_v.h, tk[("convdone", 2 * sc_i + 1)])
                        if sc_i >= 1 and half == 0:
                            s.wait_ge(c_v.h, tk[("x2done", sc_i - 1)])
                        for j in range(4):
                            c_s.inc(s.copy(mu(half * 4 + j, 0, SC),
                                           p_proj[:, j * SC:(j + 1) * SC]))
                        tk[("uevac", sc_i, half)] = c_s.n
                    block.scalar(f_uevac)
                tk[("uproj", sc_i)] = c_t.n

                if "u0" in taps and sc_i == 0:
                    def f_tap_u(g):
                        g.wait_ge(c_s.h, tk[("uevac", 0, 1)])
                        g.wait_ge(d_gp.h, d_gp.n)
                        g.dma_start(taps["u0"], u_pre[:]).then_inc(d_gp.h, 16)
                        d_gp.n += 16
                    block.gpsimd(f_tap_u)

                for p2 in range(2):
                    cc = 2 * sc_i + p2

                    def f_usq(s, sc_i=sc_i, p2=p2, cc=cc):
                        s.wait_ge(c_s.h, tk[("uevac", sc_i, 1)])
                        if sc_i >= 1:
                            s.wait_ge(c_t.h, tk[("fstat", "ln4", sc_i - 1)])
                        if stat_free is not None:
                            s.wait_ge(stat_free[0].h, stat_free[1])
                        for dtile in range(ND):
                            c_s.inc(s.activation(f_sq[:, dtile, 0:TC],
                                                 mu(dtile, p2 * TC, (p2 + 1) * TC),
                                                 ACT.Square))
                        tk[("sq", "ln2", cc)] = c_s.n
                    block.scalar(f_usq)

                    ln_emit("ln2", cc,
                            [mu(k, p2 * TC, (p2 + 1) * TC) for k in range(ND)],
                            [f_sq[:, k, 0:TC] for k in range(ND)], D,
                            (c_s, tk[("sq", "ln2", cc)]))
                    stat_free = (c_t, tk[("stat", "ln2", cc)])

                    def f_x2(v, sc_i=sc_i, p2=p2, cc=cc, st0=st0):
                        v.wait_ge(c_t.h, tk[("reps", "ln2", cc)])
                        if sc_i >= 1:
                            v.wait_ge(c_v.h, tk[("fout", sc_i - 1, ND - 1)])
                        t1 = None
                        for dtile in range(ND):
                            if t1 is not None:
                                v.wait_ge(c_v.h, t1)
                            t1 = c_v.inc(v.tensor_mul(y_tmp[:, 0:TC],
                                                      mu(dtile, p2 * TC, (p2 + 1) * TC),
                                                      p_rep[:, 0:TC]))
                            v.wait_ge(c_v.h, t1)
                            t1 = c_v.inc(v.tensor_add(y_tmp[:, 0:TC], y_tmp[:, 0:TC],
                                                      p_rep2[:, 0:TC]))
                            v.wait_ge(c_v.h, t1)
                            t1 = c_v.inc(v.tensor_add(x2s[:, dtile, p2 * TC:(p2 + 1) * TC],
                                                      xc[:, p2, dtile, :], y_tmp[:, 0:TC]))
                        tk[("x2h", sc_i, p2)] = c_v.n
                    block.vector(f_x2)
                    rep_free = (c_v, tk[("x2h", sc_i, p2)])
                tk[("x2done", sc_i)] = c_v.n

            _t = 3
            emit_ffn_pre(_t, 4)
            relu_layout(_t, c_s.n, False)
            _pos = c_v.n
            def acc_tokens_t(t_, e_, pos):
                if e_ == 0:
                    for dmg in range(4):
                        for jj in range(2):
                            tk[("accvh", t_, e_, dmg, jj)] = pos + 2 * dmg + jj + 1
                    return pos + 8
                for dmg in range(4):
                    for jj in range(2):
                        tk[("accvh", t_, e_, dmg, jj)] = pos + 2 * (2 * dmg + jj) + 1
                return pos + 16
            for e_ in range(NG):
                _pos = acc_tokens_t(_t, e_, _pos)
            for k_ in range(NG):
                if k_ >= 1:
                    emit_relu(_t, k_ - 1)
                emit_cwrep(_t, k_)
                if k_ >= 1:
                    emit_w2(_t, k_ - 1, 4)
                emit_fwa(_t, k_)
                emit_w1(_t, k_, 4)
                emit_fwb(_t, k_)
                if k_ >= 1:
                    emit_acc(_t, k_ - 1)
            emit_relu(_t, NG - 1)
            emit_w2(_t, NG - 1, 4)
            emit_acc(_t, NG - 1)
            emit_ffn_post(_t, 4)

            def f_final(sy):
                sy.wait_ge(c_o0.h, c_o0.n)
                sy.wait_ge(c_o1.h, c_o1.n)
            block.sync(f_final)

            def f_final_g(g):
                g.wait_ge(d_gp.h, d_gp.n)
                g.wait_ge(d_ce.h, d_ce.n)
            block.gpsimd(f_final_g)

    return nc


def _layout_inputs(inputs):
    import ml_dtypes
    g = {}
    g["w_in"] = np.ascontiguousarray(inputs["L0_attn_in_proj_w"])
    cw = np.asarray(inputs["L0_attn_conv_w"]).reshape(DI, 4)
    g["w_conv"] = np.ascontiguousarray(cw.reshape(NG, 128, 4).transpose(1, 0, 2))
    g["conv_b"] = np.ascontiguousarray(np.asarray(inputs["L0_attn_conv_b"]).reshape(NG, 128).T)
    g["w_xp"] = np.ascontiguousarray(
        np.asarray(inputs["L0_attn_x_proj_w"]).reshape(NG, 128, 33).transpose(1, 0, 2))
    g["dt_w"] = np.ascontiguousarray(np.asarray(inputs["L0_attn_dt_w"]).reshape(NG, 128).T)
    g["dt_b"] = np.ascontiguousarray(np.asarray(inputs["L0_attn_dt_b"]).reshape(NG, 128).T)
    g["a_log"] = np.ascontiguousarray(
        np.asarray(inputs["L0_attn_A_log"]).reshape(NG, 128, NS).transpose(1, 0, 2))
    g["d_par"] = np.ascontiguousarray(np.asarray(inputs["L0_attn_D"]).reshape(NG, 128).T)
    g["w_out"] = np.ascontiguousarray(inputs["L0_attn_out_proj_w"])
    g["r_w"] = np.ascontiguousarray(
        np.asarray(inputs["L0_ffn_router_W"]).reshape(ND, 128, NE).transpose(1, 0, 2))
    g["r_b"] = np.ascontiguousarray(np.asarray(inputs["L0_ffn_router_b"]).reshape(1, NE))
    g["s_w1"] = np.ascontiguousarray(inputs["L0_ffn_sW1"])
    g["s_b1"] = np.ascontiguousarray(np.asarray(inputs["L0_ffn_sb1"]).reshape(NH, 128).T)
    g["s_w2"] = np.ascontiguousarray(inputs["L0_ffn_sW2"])
    g["s_b2"] = np.ascontiguousarray(np.asarray(inputs["L0_ffn_sb2"]).reshape(1, D))
    g["e_w1"] = np.ascontiguousarray(inputs["L0_ffn_eW1"])
    g["e_b1"] = np.ascontiguousarray(
        np.asarray(inputs["L0_ffn_eb1"]).T.reshape(NH, 128, NE).transpose(1, 2, 0))
    g["e_w2"] = np.ascontiguousarray(inputs["L0_ffn_eW2"])
    g["e_b2"] = np.ascontiguousarray(inputs["L0_ffn_eb2"])
    g["cst"] = np.tile(np.array([[EPS, 1.0, -0.5, 0.0]], np.float32), (128, 1))
    g["ones_r"] = np.ones((1, TF), np.float32)
    g["ones_rb"] = np.ones((1, TF), ml_dtypes.bfloat16)
    g["ones_cb"] = np.ones((128, 1), ml_dtypes.bfloat16)
    g["eye"] = np.eye(128, dtype=np.float32)
    selm = np.zeros((33, 33 * 128), np.float32)
    for n in range(33):
        selm[n, n * 128:(n + 1) * 128] = 1.0
    g["sel"] = selm.astype(ml_dtypes.bfloat16)
    return g


def kernel(**inputs):
    g = _layout_inputs(inputs)
    x = np.asarray(inputs["L0_attn_x"])
    nc = build_program()
    in_maps = []
    for b in range(8):
        m = dict(g)
        m["x"] = np.ascontiguousarray(x[b].T)
        in_maps.append(m)
    res = run_bass_kernel_spmd(nc, in_maps, core_ids=list(range(8)))
    out = np.stack([np.asarray(r["out"]).T for r in res.results], axis=0)
    return np.ascontiguousarray(out.astype(np.float32))
